# version 05
# speedup vs baseline: 3.7408x; 1.0878x over previous
import numpy as np
from contextlib import ExitStack

import concourse.bass as bass
import concourse.mybir as mybir
from concourse.bass_utils import run_bass_kernel_spmd

F32 = mybir.dt.float32
BF16 = mybir.dt.bfloat16
I32 = mybir.dt.int32
AF = mybir.ActivationFunctionType
ALU = mybir.AluOpType
AX = mybir.AxisListType

NCORES = 8
T = 8192
D = 2048
BLK = 512
TPC = 1024
NTILE = TPC // 128
LN_EPS = 1e-5
RSQ128 = float(128.0 ** -0.5)
SERIAL_L0GRP = False


def core_blocks(c):
    return (c, 15 - c)


def shard_rows(c):
    b1, b2 = core_blocks(c)
    return np.r_[b1 * BLK:(b1 + 1) * BLK, b2 * BLK:(b2 + 1) * BLK]


def token_owner(b):
    return (b, 0) if b < 8 else (15 - b, 1)


class Sem:

    def __init__(self, es, nc, name):
        self.h = es.enter_context(nc.semaphore(name))
        self.n = 0

    def inc(self, inst, amount=1):
        inst.then_inc(self.h, amount)
        self.n += amount
        return self.n

    def wait(self, eng):
        eng.wait_ge(self.h, self.n)


def hadamard128():
    H = np.ones((1, 1), np.float32)
    while H.shape[0] < 128:
        H = np.block([[H, H], [H, -H]])
    return H



def emit_ln_tiles(nc, x_tiles, out_tiles, ntile, width, es, tag, out_scale=None,
                  shared=None, sems=None, pre_waits=()):
    if shared is None:
        sc = es.enter_context(nc.sbuf_tensor(f"ln_scratch_{tag}", [128, width], BF16))
        stats = es.enter_context(nc.sbuf_tensor(f"ln_stats_{tag}", [128, 8], F32))
    else:
        sc, stats = shared
    if sems is None:
        sv = Sem(es, nc, f"ln_v_{tag}")
        ss = Sem(es, nc, f"ln_s_{tag}")
    else:
        sv, ss = sems
    for i in range(ntile):
        with nc.Block() as block:
            @block.vector
            def _(vector, i=i):
                for s_, n_ in pre_waits:
                    vector.wait_ge(s_.h, n_)
                vector.tensor_reduce(stats[:, 0:1], x_tiles[i][:], axis=AX.X, op=ALU.add)

            @block.scalar
            def _(scalar, i=i):
                for s_, n_ in pre_waits:
                    scalar.wait_ge(s_.h, n_)
                ss.inc(scalar.activation(sc[:], x_tiles[i][:], AF.Square))

        with nc.Block() as block:
            @block.vector
            def _(vector, i=i):
                vector.wait_ge(ss.h, ss.n)
                sv.inc(vector.tensor_reduce(stats[:, 1:2], sc[:], axis=AX.X, op=ALU.add))
                sv.inc(vector.tensor_scalar(stats[:, 2:3], stats[:, 0:1],
                                            1.0 / width, 0.0,
                                            op0=ALU.mult, op1=ALU.add))
                vector.wait_ge(sv.h, sv.n)
                sv.inc(vector.tensor_scalar(stats[:, 3:4], stats[:, 1:2],
                                            1.0 / width, 0.0,
                                            op0=ALU.mult, op1=ALU.add))
                sv.inc(vector.tensor_tensor(stats[:, 6:7], stats[:, 2:3],
                                            stats[:, 2:3], op=ALU.mult))
                vector.wait_ge(sv.h, sv.n)
                sv.inc(vector.tensor_tensor(stats[:, 3:4], stats[:, 3:4],
                                            stats[:, 6:7], op=ALU.subtract))
                vector.wait_ge(sv.h, sv.n)
                sv.inc(vector.tensor_scalar(stats[:, 3:4], stats[:, 3:4], LN_EPS, 0.0,
                                            op0=ALU.add, op1=ALU.add))

        with nc.Block() as block:
            @block.scalar
            def _(scalar, i=i):
                scalar.wait_ge(sv.h, sv.n)
                ss.inc(scalar.activation(stats[:, 4:5], stats[:, 3:4], AF.Sqrt))

        with nc.Block() as block:
            @block.vector
            def _(vector, i=i):
                vector.wait_ge(ss.h, ss.n)
                sv.inc(vector.reciprocal(stats[:, 5:6], stats[:, 4:5]))
                vector.wait_ge(sv.h, sv.n)
                if out_scale is not None:
                    sv.inc(vector.tensor_scalar(stats[:, 5:6], stats[:, 5:6],
                                                out_scale, 0.0,
                                                op0=ALU.mult, op1=ALU.add))
                    vector.wait_ge(sv.h, sv.n)
                sv.inc(vector.tensor_scalar(out_tiles[i][:], x_tiles[i][:],
                                            stats[:, 2:3], stats[:, 5:6],
                                            op0=ALU.subtract, op1=ALU.mult))


def emit_transpose_to_fm(nc, sems, src_tiles, dst, ntile, nchunk, psT2, id_bf,
                         dst_col_tokens, src_col0=0, dst_tile_idx=None,
                         pre_waits=()):
    st, sv = sems
    bT, bV = st.n, sv.n
    jobs = [(i, f) for i in range(ntile) for f in range(nchunk)]
    with nc.Block() as block:
        @block.tensor
        def _(tensor):
            for s, n in pre_waits:
                tensor.wait_ge(s.h, n)
            for j, (i, f) in enumerate(jobs):
                if j >= 2:
                    tensor.wait_ge(sv.h, bV + j - 1)
                st.inc(tensor.transpose(
                    psT2[j % 2][:, 0:128],
                    src_tiles[i][:, src_col0 + 128 * f:src_col0 + 128 * (f + 1)],
                    id_bf[:]))

        @block.vector
        def _(vector):
            for j, (i, f) in enumerate(jobs):
                di = i if dst_tile_idx is None else dst_tile_idx
                vector.wait_ge(st.h, bT + j + 1)
                sv.inc(vector.tensor_copy(
                    dst[:, f * dst_col_tokens + di * 128:
                           f * dst_col_tokens + (di + 1) * 128],
                    psT2[j % 2][:, 0:128]))



class NCP:

    def __init__(self, nc, pfx):
        self._nc = nc
        self._pfx = pfx

    def __getattr__(self, k):
        return getattr(self._nc, k)

    def sbuf_tensor(self, name, *a, **k):
        return self._nc.sbuf_tensor(self._pfx + name, *a, **k)

    def psum_tensor(self, name, *a, **k):
        return self._nc.psum_tensor(self._pfx + name, *a, **k)

    def semaphore(self, name=None, num=None):
        return self._nc.semaphore(self._pfx + (name or "s"))

    def dram_tensor(self, name, *a, **k):
        return self._nc.dram_tensor(self._pfx + name, *a, **k)


def _io_funcs(nc, io):
    def IN(name, shape, dt):
        if io is not None and name in io:
            return io[name]
        return nc.dram_tensor(name, shape, dt, kind="ExternalInput").ap()

    def OUT(name, shape, dt):
        if io is not None and name in io:
            return io[name]
        return nc.dram_tensor(name, shape, dt, kind="ExternalOutput").ap()
    return IN, OUT


def emit_all_wait(nc, sems_counts):
    with nc.Block() as block:
        for engname in ("sync", "tensor", "vector", "scalar", "gpsimd"):
            def mk(engname=engname):
                def _(eng):
                    for s_, n_ in sems_counts:
                        eng.wait_ge(s_.h, n_)
                return _
            getattr(block, engname)(mk())



def build_l0attn(debug=False, ncp=None, io=None):
    nc = ncp if ncp is not None else bass.Bass("TRN2", target_bir_lowering=False)
    IN, OUT = _io_funcs(nc, io)
    xs = IN("xs", [TPC, D], F32)
    wqkvT = IN("wqkvT", [D, 3 * D], F32)
    identf = IN("identf", [128, 128], F32)
    l0mask = IN("l0mask", [128, 128], F32)
    obuf = OUT("obuf", [TPC * 16, 128], F32)

    es = ExitStack()
    sd = Sem(es, nc, "sd")
    sst = Sem(es, nc, "sst")
    lnv = Sem(es, nc, "lnv")
    lns = Sem(es, nc, "lns")
    trt = Sem(es, nc, "trt")
    trv = Sem(es, nc, "trv")
    scv = Sem(es, nc, "scv")
    st2 = Sem(es, nc, "st2")
    sv2 = Sem(es, nc, "sv2")
    sT = Sem(es, nc, "g_sT")
    sE = Sem(es, nc, "g_sE")
    sV_ = Sem(es, nc, "g_sV")
    sO = Sem(es, nc, "g_sO")
    sR = Sem(es, nc, "g_sR")

    id_f = es.enter_context(nc.sbuf_tensor("id_f", [128, 128], F32))
    id_bf = es.enter_context(nc.sbuf_tensor("id_bf", [128, 128], BF16))
    mask_f = es.enter_context(nc.sbuf_tensor("mask_f", [128, 128], F32))
    mask_bf = es.enter_context(nc.sbuf_tensor("mask_bf", [128, 128], BF16))
    q_fm = es.enter_context(nc.sbuf_tensor("q_fm", [128, 16 * TPC], BF16))
    k_fm = es.enter_context(nc.sbuf_tensor("k_fm", [128, 16 * TPC], BF16))
    v_fm = es.enter_context(nc.sbuf_tensor("v_fm", [128, 16 * TPC], BF16))
    tT = es.enter_context(nc.sbuf_tensor("tT", [128, 16 * TPC], BF16))
    psT2 = [es.enter_context(nc.psum_tensor(f"psT{i}", [128, 1024], BF16))
            for i in range(2)]

    with nc.Block() as block:
        @block.sync
        def _(sync):
            sd.inc(sync.dma_start(out=id_f[:], in_=identf[:, :]), 16)
            sd.inc(sync.dma_start(out=mask_f[:], in_=l0mask[:, :]), 16)
            sd.wait(sync)

    with nc.Block() as block:
        @block.vector
        def _(vector):
            vector.tensor_copy(id_bf[:], id_f[:])
            scv.inc(vector.tensor_copy(mask_bf[:], mask_f[:]))

    with ExitStack() as es1:
        x_sb = [es1.enter_context(nc.sbuf_tensor(f"x{i}", [128, D], F32))
                for i in range(2)]
        t_bf = [es1.enter_context(nc.sbuf_tensor(f"t{i}", [128, D], BF16))
                for i in range(2)]
        ln_sc = es1.enter_context(nc.sbuf_tensor("ln_sc", [128, D], BF16))
        ln_stats = es1.enter_context(nc.sbuf_tensor("ln_stats", [128, 8], F32))
        for i in range(NTILE):
            with nc.Block() as block:
                @block.sync
                def _(sync, i=i):
                    sd.inc(sync.dma_start(out=x_sb[i % 2][:],
                                          in_=xs[128 * i:128 * (i + 1), :]), 16)
                    sd.wait(sync)

            emit_ln_tiles(nc, [x_sb[i % 2]], [t_bf[i % 2]], 1, D, es1, f"l0a{i}",
                          shared=(ln_sc, ln_stats), sems=(lnv, lns))
            emit_transpose_to_fm(nc, (trt, trv), [t_bf[i % 2]], tT, 1, 16,
                                 psT2, id_bf, TPC, dst_tile_idx=i,
                                 pre_waits=[(lnv, lnv.n)])

    with ExitStack() as es2:
        wstage = [es2.enter_context(nc.sbuf_tensor(f"ws{i}", [128, 512], F32))
                  for i in range(16)]
        wbf = [es2.enter_context(nc.sbuf_tensor(f"wb{i}", [128, 512], BF16))
               for i in range(16)]
        ps = [es2.enter_context(nc.psum_tensor(f"psq{i}", [128, 512], F32))
              for i in range(2)]
        for fc in range(12):
            with nc.Block() as block:
                @block.sync
                def _(sync, fc=fc):
                    sync.wait_ge(scv.h, scv.n)
                    for i in range(16):
                        sd.inc(sync.dma_start(
                            out=wstage[i][:],
                            in_=wqkvT[128 * i:128 * (i + 1), 512 * fc:512 * (fc + 1)]), 16)
                    sd.wait(sync)

            with nc.Block() as block:
                @block.vector
                def _(vector, fc=fc):
                    if fc == 0:
                        vector.wait_ge(trv.h, trv.n)
                    vector.wait_ge(st2.h, st2.n)
                    for i in range(16):
                        scv.inc(vector.tensor_copy(wbf[i][:], wstage[i][:]))

            sv2_base = sv2.n
            scv_goal = scv.n
            with nc.Block() as block:
                @block.tensor
                def _(tensor, fc=fc, sv2_base=sv2_base, scv_goal=scv_goal):
                    tensor.wait_ge(scv.h, scv_goal)
                    if fc == 0:
                        tensor.wait_ge(trv.h, trv.n)
                    for j in range(8):
                        sub, tc = j // 2, j % 2
                        if sv2_base + j - 1 > 0:
                            tensor.wait_ge(sv2.h, sv2_base + j - 1)
                        for i in range(16):
                            mm = tensor.matmul(
                                ps[j % 2][:],
                                wbf[i][:, 128 * sub:128 * (sub + 1)],
                                tT[:, TPC * i + 512 * tc: TPC * i + 512 * (tc + 1)],
                                start=(i == 0), stop=(i == 15))
                        st2.inc(mm)

                @block.vector
                def _(vector, fc=fc):
                    for j in range(8):
                        sub, tc = j // 2, j % 2
                        fo = 512 * fc + 128 * sub
                        vector.wait_ge(st2.h, st2.n - 8 + j + 1)
                        if fo < D:
                            h = fo // 128
                            dst = q_fm[:].rearrange("p (n h) -> p h n", h=16)
                            sv2.inc(vector.tensor_scalar(
                                dst[:, h, 512 * tc:512 * (tc + 1)],
                                ps[j % 2][:], RSQ128, 0.0, op0=ALU.mult, op1=ALU.add))
                        elif fo < 2 * D:
                            h = (fo - D) // 128
                            dst = k_fm[:].rearrange("p (n h) -> p h n", h=16)
                            sv2.inc(vector.tensor_copy(
                                dst[:, h, 512 * tc:512 * (tc + 1)], ps[j % 2][:]))
                        else:
                            h = (fo - 2 * D) // 128
                            dst = v_fm[:].rearrange("p (n h) -> p h n", h=16)
                            sv2.inc(vector.tensor_copy(
                                dst[:, h, 512 * tc:512 * (tc + 1)], ps[j % 2][:]))

    if debug:
        with ExitStack() as esd:
            dbg_tT = OUT("dbg_tT", [128, 16 * TPC], F32)
            dbg_q = OUT("dbg_q", [128, 16 * TPC], F32)
            dbg_k = OUT("dbg_k", [128, 16 * TPC], F32)
            dbg_v = OUT("dbg_v", [128, 16 * TPC], F32)
            dstage = esd.enter_context(nc.sbuf_tensor("dstage", [128, 4096], F32))
            for name, src, dst in (("tT", tT, dbg_tT), ("q", q_fm, dbg_q),
                                   ("k", k_fm, dbg_k), ("v", v_fm, dbg_v)):
                for part in range(4):
                    with nc.Block() as block:
                        @block.vector
                        def _(vector, src=src, part=part):
                            vector.wait_ge(sv2.h, sv2.n)
                            vector.wait_ge(sst.h, sst.n)
                            scv.inc(vector.tensor_copy(
                                dstage[:], src[:, 4096 * part:4096 * (part + 1)]))

                    with nc.Block() as block:
                        @block.sync
                        def _(sync, dst=dst, part=part):
                            sync.wait_ge(scv.h, scv.n)
                            sst.inc(sync.dma_start(
                                out=dst[:, 4096 * part:4096 * (part + 1)],
                                in_=dstage[:]), 16)
                            sst.wait(sync)

    with nc.Block() as block:
        @block.vector
        def _(vector):
            sv2.inc(vector.tensor_copy(id_f[0:1, 0:8], id_f[0:1, 0:8]))

    with ExitStack() as es3:
        og = [es3.enter_context(nc.sbuf_tensor(f"og{i}", [128, 2048], F32))
              for i in range(2)]
        p_bf = [es3.enter_context(nc.sbuf_tensor(f"pb{i}", [128, 128], BF16))
                for i in range(2)]
        vg = [es3.enter_context(nc.sbuf_tensor(f"vg{i}", [128, 129], BF16))
              for i in range(2)]
        rec1 = es3.enter_context(nc.sbuf_tensor("rec1", [128, 1], F32))
        ps_s = [es3.enter_context(nc.psum_tensor(f"pss{i}", [128, 512], F32))
                for i in range(2)]
        ps_o = [es3.enter_context(nc.psum_tensor(f"pso{i}", [128, 512], F32))
                for i in range(2)]
        psV = [es3.enter_context(nc.psum_tensor(f"psV{i}", [128, 1024], BF16))
               for i in range(2)]
        with nc.Block() as block:
            @block.vector
            def _(vector):
                vector.memset(vg[0][:, 128:129], 1.0)
                sV_.inc(vector.memset(vg[1][:, 128:129], 1.0))

        def grp_ap(t, nb):
            return t[:, 16 * nb:16 * (nb + 8)]

        qkv_done = sv2.n
        ones_done = sV_.n
        if SERIAL_L0GRP:
            for tt in range(NTILE):
                for g in range(16):
                    nb = 128 * tt + 8 * g
                    pp = g % 2
                    with nc.Block() as block:
                        @block.tensor
                        def _(tensor, nb=nb, pp=pp, tt=tt, g=g):
                            if tt == 0 and g == 0:
                                tensor.wait_ge(sv2.h, qkv_done)
                            sT.inc(tensor.matmul(ps_s[pp][:, 0:128], grp_ap(k_fm, nb),
                                                 grp_ap(q_fm, nb), start=True, stop=True))
                            sT.inc(tensor.transpose(psV[pp][:, 0:128], grp_ap(v_fm, nb),
                                                    id_bf[:]))
                            tensor.wait_ge(sT.h, sT.n)
                    with nc.Block() as block:
                        @block.scalar
                        def _(scalar, pp=pp):
                            sE.inc(scalar.activation(p_bf[pp][:], ps_s[pp][:, 0:128], AF.Exp))
                    with nc.Block() as block:
                        @block.vector
                        def _(vector, pp=pp, tt=tt, g=g):
                            if g == 0 and tt >= 2:
                                vector.wait_ge(sst.h, 16 * (tt - 1))
                            vector.tensor_tensor(p_bf[pp][:], p_bf[pp][:], mask_bf[:], op=ALU.mult)
                            sV_.inc(vector.tensor_copy(vg[pp][:, 0:128], psV[pp][:, 0:128]))
                    with nc.Block() as block:
                        @block.tensor
                        def _(tensor, pp=pp):
                            sT.inc(tensor.matmul(ps_o[pp][:, 0:129], p_bf[pp][:],
                                                 vg[pp][:, 0:129], start=True, stop=True))
                            tensor.wait_ge(sT.h, sT.n)
                    with nc.Block() as block:
                        @block.vector
                        def _(vector, pp=pp, tt=tt, g=g):
                            sR.inc(vector.reciprocal(rec1[:], ps_o[pp][:, 128:129]))
                            vector.wait_ge(sR.h, sR.n)
                            sO.inc(vector.tensor_scalar(
                                og[tt % 2][:, 128 * g:128 * (g + 1)],
                                ps_o[pp][:, 0:128], rec1[:], 0.0,
                                op0=ALU.mult, op1=ALU.add))
                with nc.Block() as block:
                    @block.sync
                    def _(sync, tt=tt):
                        sync.wait_ge(sO.h, sO.n)
                        sst.inc(sync.dma_start(
                            out=obuf[2048 * tt:2048 * (tt + 1), :].rearrange(
                                "(g p) d -> p g d", p=128),
                            in_=og[tt % 2][:].rearrange("p (g d) -> p g d", g=16)), 16)
        for tt in range([] if SERIAL_L0GRP else range(NTILE)) if False else (range(0) if SERIAL_L0GRP else range(NTILE)):
            bT, bE, bV, bO = sT.n, sE.n, sV_.n, sO.n
            with nc.Block() as block:
                @block.tensor
                def _(tensor, tt=tt, bT=bT, bE=bE, bV=bV, bO=bO):
                    if tt == 0:
                        tensor.wait_ge(sv2.h, qkv_done)
                        tensor.wait_ge(sV_.h, ones_done)
                    for g in range(16):
                        nb = 128 * tt + 8 * g
                        pp = g % 2
                        if g >= 2:
                            tensor.wait_ge(sE.h, bE + g - 1)
                            tensor.wait_ge(sV_.h, bV + g - 1)
                            tensor.wait_ge(sO.h, bO + g - 1)
                        sT.inc(tensor.matmul(ps_s[pp][:, 0:128], grp_ap(k_fm, nb),
                                             grp_ap(q_fm, nb), start=True, stop=True))
                        sT.inc(tensor.transpose(psV[pp][:, 0:128], grp_ap(v_fm, nb),
                                                id_bf[:]))
                        tensor.wait_ge(sE.h, bE + g + 1)
                        tensor.wait_ge(sV_.h, bV + g + 1)
                        sT.inc(tensor.matmul(ps_o[pp][:, 0:129], p_bf[pp][:],
                                             vg[pp][:, 0:129], start=True, stop=True))

                @block.scalar
                def _(scalar, tt=tt, bT=bT, bE=bE):
                    for g in range(16):
                        pp = g % 2
                        scalar.wait_ge(sT.h, bT + 3 * g + 1)
                        sE.inc(scalar.activation(p_bf[pp][:], ps_s[pp][:, 0:128], AF.Exp))

                @block.vector
                def _(vector, tt=tt, bT=bT, bE=bE):
                    for g in range(16):
                        pp = g % 2
                        if g == 0 and tt >= 2:
                            vector.wait_ge(sst.h, sst.n)
                        vector.wait_ge(sE.h, bE + g + 1)
                        vector.tensor_tensor(p_bf[pp][:], p_bf[pp][:], mask_bf[:], op=ALU.mult)
                        vector.wait_ge(sT.h, bT + 3 * g + 2)
                        sV_.inc(vector.tensor_copy(vg[pp][:, 0:128], psV[pp][:, 0:128]))
                        vector.wait_ge(sT.h, bT + 3 * g + 3)
                        sR.inc(vector.reciprocal(rec1[:], ps_o[pp][:, 128:129]))
                        vector.wait_ge(sR.h, sR.n)
                        sO.inc(vector.tensor_scalar(
                            og[tt % 2][:, 128 * g:128 * (g + 1)],
                            ps_o[pp][:, 0:128], rec1[:], 0.0,
                            op0=ALU.mult, op1=ALU.add))

            with nc.Block() as block:
                @block.sync
                def _(sync, tt=tt):
                    sync.wait_ge(sO.h, sO.n)
                    sst.inc(sync.dma_start(
                        out=obuf[2048 * tt:2048 * (tt + 1), :].rearrange(
                            "(g p) d -> p g d", p=128),
                        in_=og[tt % 2][:].rearrange("p (g d) -> p g d", g=16)), 16)

        with nc.Block() as block:
            @block.sync
            def _(sync):
                sst.wait(sync)
    es.close()
    return nc



def build_l0proj(ncp=None, io=None):
    nc = ncp if ncp is not None else bass.Bass("TRN2", target_bir_lowering=False)
    IN, OUT = _io_funcs(nc, io)
    Pm = IN("Pm", [TPC, D], F32)
    xs = IN("xs", [TPC, D], F32)
    wprojT = IN("wprojT", [D, D], F32)
    identf = IN("identf", [128, 128], F32)
    x1 = OUT("x1", [TPC, D], F32)

    es = ExitStack()
    sd = Sem(es, nc, "sd")
    sst = Sem(es, nc, "sst")
    lnv = Sem(es, nc, "lnv")
    lns = Sem(es, nc, "lns")
    trt = Sem(es, nc, "trt")
    trv = Sem(es, nc, "trv")
    scv = Sem(es, nc, "scv")
    smm = Sem(es, nc, "smm")
    sev = Sem(es, nc, "sev")

    id_f = es.enter_context(nc.sbuf_tensor("id_f", [128, 128], F32))
    id_bf = es.enter_context(nc.sbuf_tensor("id_bf", [128, 128], BF16))
    PT = es.enter_context(nc.sbuf_tensor("PT", [128, 16 * TPC], BF16))
    wpb = [es.enter_context(nc.sbuf_tensor(f"wpb{i}", [128, D], BF16))
           for i in range(16)]
    psT2 = [es.enter_context(nc.psum_tensor(f"psT{i}", [128, 1024], BF16))
            for i in range(2)]
    ps = [es.enter_context(nc.psum_tensor(f"psq{i}", [128, 512], F32))
          for i in range(2)]

    with nc.Block() as block:
        @block.sync
        def _(sync):
            sd.inc(sync.dma_start(out=id_f[:], in_=identf[:, :]), 16)
            sd.wait(sync)

    with nc.Block() as block:
        @block.vector
        def _(vector):
            scv.inc(vector.tensor_copy(id_bf[:], id_f[:]))

    with ExitStack() as esw:
        wst = [esw.enter_context(nc.sbuf_tensor(f"wst{i}", [128, D], F32))
               for i in range(4)]
        for i in range(16):
            with nc.Block() as block:
                @block.sync
                def _(sync, i=i):
                    if i >= 4:
                        sync.wait_ge(scv.h, scv.n)
                    sd.inc(sync.dma_start(out=wst[i % 4][:],
                                          in_=wprojT[128 * i:128 * (i + 1), :]), 16)
                    sd.wait(sync)

            with nc.Block() as block:
                @block.vector
                def _(vector, i=i):
                    scv.inc(vector.tensor_copy(wpb[i][:], wst[i % 4][:]))

        pst = [esw.enter_context(nc.sbuf_tensor(f"pst{i}", [128, D], F32))
               for i in range(2)]
        pbf = [esw.enter_context(nc.sbuf_tensor(f"pbf{i}", [128, D], BF16))
               for i in range(2)]
        for i in range(NTILE):
            with nc.Block() as block:
                @block.sync
                def _(sync, i=i):
                    if i >= 2:
                        sync.wait_ge(trt.h, 16 * (i - 1))
                    sd.inc(sync.dma_start(out=pst[i % 2][:],
                                          in_=Pm[128 * i:128 * (i + 1), :]), 16)
                    sd.wait(sync)

            with nc.Block() as block:
                @block.vector
                def _(vector, i=i):
                    scv.inc(vector.tensor_copy(pbf[i % 2][:], pst[i % 2][:]))

            emit_transpose_to_fm(nc, (trt, trv), [pbf[i % 2]], PT, 1, 16,
                                 psT2, id_bf, TPC, dst_tile_idx=i,
                                 pre_waits=[(scv, scv.n)])

    with ExitStack() as es2:
        y_sb = es2.enter_context(nc.sbuf_tensor("y_sb", [128, D], F32))
        lny = es2.enter_context(nc.sbuf_tensor("lny", [128, D], F32))
        xt = [es2.enter_context(nc.sbuf_tensor(f"xt{i}", [128, D], F32))
              for i in range(2)]
        o_sb = [es2.enter_context(nc.sbuf_tensor(f"osb{i}", [128, D], F32))
                for i in range(2)]
        ln_sc = es2.enter_context(nc.sbuf_tensor("ln_sc", [128, D], BF16))
        ln_stats = es2.enter_context(nc.sbuf_tensor("ln_stats", [128, 8], F32))
        for tt in range(NTILE):
            with nc.Block() as block:
                @block.sync
                def _(sync, tt=tt):
                    sd.inc(sync.dma_start(out=xt[tt % 2][:],
                                          in_=xs[128 * tt:128 * (tt + 1), :]), 16)
                    sd.wait(sync)

            bmm = smm.n
            with nc.Block() as block:
                @block.tensor
                def _(tensor, tt=tt, bmm=bmm):
                    if tt == 0:
                        tensor.wait_ge(trv.h, trv.n)
                        tensor.wait_ge(scv.h, scv.n)
                    for fo in range(4):
                        if bmm + fo - 1 > 0:
                            tensor.wait_ge(sev.h, bmm + fo - 1)
                        for i in range(16):
                            mm = tensor.matmul(
                                ps[fo % 2][:],
                                PT[:, TPC * i + 128 * tt:TPC * i + 128 * (tt + 1)],
                                wpb[i][:, 512 * fo:512 * (fo + 1)],
                                start=(i == 0), stop=(i == 15))
                        smm.inc(mm)

                @block.vector
                def _(vector, tt=tt, bmm=bmm):
                    for fo in range(4):
                        vector.wait_ge(smm.h, bmm + fo + 1)
                        sev.inc(vector.tensor_copy(y_sb[:, 512 * fo:512 * (fo + 1)],
                                                   ps[fo % 2][:]))

            emit_ln_tiles(nc, [y_sb], [lny], 1, D, es2, f"lnb{tt}",
                          shared=(ln_sc, ln_stats), sems=(lnv, lns),
                          pre_waits=[(sev, sev.n)])
            with nc.Block() as block:
                @block.vector
                def _(vector, tt=tt):
                    if tt >= 2:
                        vector.wait_ge(sst.h, 16 * (tt - 1))
                    sev.inc(vector.tensor_tensor(o_sb[tt % 2][:], xt[tt % 2][:],
                                                 lny[:], op=ALU.add))

            with nc.Block() as block:
                @block.sync
                def _(sync, tt=tt):
                    sync.wait_ge(sev.h, sev.n)
                    sst.inc(sync.dma_start(out=x1[128 * tt:128 * (tt + 1), :],
                                           in_=o_sb[tt % 2][:]), 16)

        with nc.Block() as block:
            @block.sync
            def _(sync):
                sst.wait(sync)
    es.close()
    return nc



RSQ2048 = float(2048.0 ** -0.5)
RSQ8192 = float(8192.0 ** -0.5)


def emit_butterflies(nc, vector, buf, nchunk, CC, s_edge, last_out=None):
    s = 1
    while s < nchunk:
        na = nchunk // (2 * s)
        v4 = buf[:].rearrange("p (a b c) -> p a b c", a=na, b=2)
        A = v4[:, :, 0, :]
        B = v4[:, :, 1, :]
        if s == nchunk // 2 and last_out is not None:
            o4 = last_out[:].rearrange("p (a b c) -> p a b c", a=na, b=2)
            vector.tensor_tensor(o4[:, :, 0, :], A, B, op=ALU.add)
            s_edge.inc(vector.tensor_tensor(o4[:, :, 1, :], A, B, op=ALU.subtract))
            vector.wait_ge(s_edge.h, s_edge.n)
        else:
            s_edge.inc(vector.tensor_tensor(A, A, B, op=ALU.add))
            vector.wait_ge(s_edge.h, s_edge.n)
            s_edge.inc(vector.scalar_tensor_tensor(B, B, -2.0, A,
                                                   op0=ALU.mult, op1=ALU.add))
            vector.wait_ge(s_edge.h, s_edge.n)
        s *= 2


def build_l0ffn(ncp=None, io=None):
    nc = ncp if ncp is not None else bass.Bass("TRN2", target_bir_lowering=False)
    IN, OUT = _io_funcs(nc, io)
    x1 = IN("x1", [TPC, D], F32)
    wupT = IN("wupT", [D, 8192], F32)
    wgateT = IN("wgateT", [D, 8192], F32)
    wdownT = IN("wdownT", [8192, D], F32)
    su_ug_i = IN("su_ug", [128, 16], F32)
    su_dp_i = IN("su_dp", [128, 64], F32)
    wsu_i = IN("wsu", [128, 64], F32)
    wsg_i = IN("wsg", [128, 64], F32)
    wdp_i = IN("wdp", [1, D], F32)
    Hf = IN("Hf", [128, 128], F32)
    identf = IN("identf", [128, 128], F32)
    x2 = OUT("x2", [TPC, D], F32)
    ubf_d = nc.dram_tensor("ubf_d", [D, 8192], BF16).ap()
    gbf_d = nc.dram_tensor("gbf_d", [D, 8192], BF16).ap()
    dbf_d = nc.dram_tensor("dbf_d", [8192, D], BF16).ap()

    es = ExitStack()
    sd = Sem(es, nc, "sd")
    sst = Sem(es, nc, "sst")
    lnv = Sem(es, nc, "lnv")
    lns = Sem(es, nc, "lns")
    trt = Sem(es, nc, "trt")
    trv = Sem(es, nc, "trv")
    scv = Sem(es, nc, "scv")
    smm = Sem(es, nc, "smm")
    sev = Sem(es, nc, "sev")
    sss = Sem(es, nc, "sss")
    swd = Sem(es, nc, "swd")

    id_f = es.enter_context(nc.sbuf_tensor("id_f", [128, 128], F32))
    id_bf = es.enter_context(nc.sbuf_tensor("id_bf", [128, 128], BF16))
    H_sb = es.enter_context(nc.sbuf_tensor("H_sb", [128, 128], F32))
    H_bf = es.enter_context(nc.sbuf_tensor("H_bf", [128, 128], BF16))
    hsu = es.enter_context(nc.sbuf_tensor("hsu", [128, 16 * 128], BF16))
    su_ug = es.enter_context(nc.sbuf_tensor("su_ug_s", [128, 16], F32))
    su_dp = es.enter_context(nc.sbuf_tensor("su_dp_s", [128, 64], F32))
    wsu = es.enter_context(nc.sbuf_tensor("wsu_s", [128, 64], F32))
    wsg = es.enter_context(nc.sbuf_tensor("wsg_s", [128, 64], F32))
    wdp_row = es.enter_context(nc.sbuf_tensor("wdp_row", [1, D], F32))
    wdp_bc = es.enter_context(nc.sbuf_tensor("wdp_bc", [128, D], F32))
    ones1 = es.enter_context(nc.sbuf_tensor("ones1", [1, 128], F32))
    u_bf = es.enter_context(nc.sbuf_tensor("u_bf", [128, 16 * TPC], BF16))

    with nc.Block() as block:
        @block.sync
        def _(sync):
            for dst, src in ((id_f, identf), (H_sb, Hf)):
                sd.inc(sync.dma_start(out=dst[:], in_=src[:, :]), 16)
            for dst, src in ((su_ug, su_ug_i), (su_dp, su_dp_i), (wsu, wsu_i),
                             (wsg, wsg_i)):
                sd.inc(sync.dma_start(out=dst[:], in_=src[:, :]), 16)
            sd.inc(sync.dma_start(out=wdp_row[:], in_=wdp_i[:, :]), 16)
            sd.wait(sync)

    with nc.Block() as block:
        @block.vector
        def _(vector):
            vector.tensor_copy(id_bf[:], id_f[:])
            vector.tensor_copy(H_bf[:], H_sb[:])
            scv.inc(vector.memset(ones1[:], 1.0))

    with ExitStack() as es0:
        psb = es0.enter_context(nc.psum_tensor("psb", [128, 512], F32))
        bev0 = sev.n
        with nc.Block() as block:
            @block.tensor
            def _(tensor, bev0=bev0):
                tensor.wait_ge(scv.h, scv.n)
                for j in range(4):
                    if j > 0:
                        tensor.wait_ge(sev.h, bev0 + j)
                    smm.inc(tensor.matmul(psb[:], ones1[:, :],
                                          wdp_row[:, 512 * j:512 * (j + 1)],
                                          start=True, stop=True))

            @block.vector
            def _(vector, bev0=bev0):
                base = smm.n - 4
                for j in range(4):
                    vector.wait_ge(smm.h, base + j + 1)
                    sev.inc(vector.tensor_scalar(
                        wdp_bc[:, 512 * j:512 * (j + 1)], psb[:], RSQ8192, 0.0,
                        op0=ALU.mult, op1=ALU.add))
                for c in range(16):
                    scv.inc(vector.tensor_scalar(
                        hsu[:, 128 * c:128 * (c + 1)], H_bf[:],
                        su_ug[:, c:c + 1], 0.0, op0=ALU.mult, op1=ALU.add))

    with ExitStack() as esw:
        wst = [esw.enter_context(nc.sbuf_tensor(f"cwst{i}", [128, 4096], F32))
               for i in range(2)]
        wcv = [esw.enter_context(nc.sbuf_tensor(f"cwcv{i}", [128, 4096], BF16))
               for i in range(2)]
        sldA = Sem(es, nc, "cw_ldA")
        sldB = Sem(es, nc, "cw_ldB")
        jobs = []
        for r in range(16):
            for hh in range(2):
                jobs.append((wupT, ubf_d, 128 * r, 4096 * hh, 4096))
                jobs.append((wgateT, gbf_d, 128 * r, 4096 * hh, 4096))
        for r in range(64):
            jobs.append((wdownT, dbf_d, 128 * r, 0, 2048))
        bcv = scv.n
        bwd = swd.n
        with nc.Block() as block:
            @block.sync
            def _(sync):
                for j, (srcw, dstw, r0, c0, w) in enumerate(jobs):
                    ld = sldA if j % 2 == 0 else sldB
                    if j >= 2:
                        sync.wait_ge(scv.h, bcv + j - 1)
                    sync.wait_ge(ld.h, ld.n)
                    ld.inc(sync.dma_start(out=wst[j % 2][:, 0:w],
                                          in_=srcw[r0:r0 + 128, c0:c0 + w]), 16)
                    if j >= 2:
                        sj = j - 2
                        srcw2, dstw2, r2, c2, w2 = jobs[sj]
                        sync.wait_ge(scv.h, bcv + sj + 1)
                        sync.wait_ge(swd.h, swd.n)
                        swd.inc(sync.dma_start(out=dstw2[r2:r2 + 128, c2:c2 + w2],
                                               in_=wcv[sj % 2][:, 0:w2]), 16)
                for j in (len(jobs) - 2, len(jobs) - 1):
                    srcw2, dstw2, r2, c2, w2 = jobs[j]
                    sync.wait_ge(scv.h, bcv + j + 1)
                    sync.wait_ge(swd.h, swd.n)
                    swd.inc(sync.dma_start(out=dstw2[r2:r2 + 128, c2:c2 + w2],
                                           in_=wcv[j % 2][:, 0:w2]), 16)
                sync.wait_ge(swd.h, swd.n)

            @block.vector
            def _(vector):
                for j, (srcw, dstw, r0, c0, w) in enumerate(jobs):
                    ld = sldA if j % 2 == 0 else sldB
                    vector.wait_ge(ld.h, 16 * (j // 2 + 1))
                    if j >= 2:
                        vector.wait_ge(swd.h, bwd + 16 * (j - 1))
                    scv.inc(vector.tensor_copy(wcv[j % 2][:, 0:w], wst[j % 2][:, 0:w]))

    emit_all_wait(nc, [(swd, swd.n)])

    with ExitStack() as es1:
        tT = es1.enter_context(nc.sbuf_tensor("tT", [128, 16 * TPC], BF16))
        u1 = es1.enter_context(nc.sbuf_tensor("u1", [128, 16 * TPC], F32))
        psT2 = [es1.enter_context(nc.psum_tensor(f"psT{i}", [128, 1024], BF16))
                for i in range(2)]
        x_sb = [es1.enter_context(nc.sbuf_tensor(f"cx{i}", [128, D], F32))
                for i in range(2)]
        t_bf = [es1.enter_context(nc.sbuf_tensor(f"ct{i}", [128, D], BF16))
                for i in range(2)]
        ln_sc = es1.enter_context(nc.sbuf_tensor("ln_sc", [128, D], BF16))
        ln_stats = es1.enter_context(nc.sbuf_tensor("ln_stats", [128, 8], F32))
        for i in range(NTILE):
            with nc.Block() as block:
                @block.sync
                def _(sync, i=i):
                    sd.inc(sync.dma_start(out=x_sb[i % 2][:],
                                          in_=x1[128 * i:128 * (i + 1), :]), 16)
                    sd.wait(sync)

            emit_ln_tiles(nc, [x_sb[i % 2]], [t_bf[i % 2]], 1, D, es1, f"c{i}",
                          shared=(ln_sc, ln_stats), sems=(lnv, lns))
            emit_transpose_to_fm(nc, (trt, trv), [t_bf[i % 2]], tT, 1, 16,
                                 psT2, id_bf, TPC, dst_tile_idx=i,
                                 pre_waits=[(lnv, lnv.n)])

        psU = [es1.enter_context(nc.psum_tensor(f"psU{i}", [128, 512], F32))
               for i in range(2)]
        bmm = smm.n
        bev = sev.n
        with nc.Block() as block:
            @block.tensor
            def _(tensor, bmm=bmm, bev=bev):
                tensor.wait_ge(trv.h, trv.n)
                tensor.wait_ge(scv.h, scv.n)
                for j in range(32):
                    c, tc = j // 2, j % 2
                    if bev + j - 1 > 0:
                        tensor.wait_ge(sev.h, bev + j - 1)
                    smm.inc(tensor.matmul(
                        psU[j % 2][:], hsu[:, 128 * c:128 * (c + 1)],
                        tT[:, TPC * c + 512 * tc:TPC * c + 512 * (tc + 1)],
                        start=True, stop=True))

            @block.vector
            def _(vector, bmm=bmm, bev=bev):
                for j in range(32):
                    c, tc = j // 2, j % 2
                    vector.wait_ge(smm.h, bmm + j + 1)
                    sev.inc(vector.tensor_scalar(
                        u1[:, TPC * c + 512 * tc:TPC * c + 512 * (tc + 1)],
                        psU[j % 2][:], RSQ2048, 0.0, op0=ALU.mult, op1=ALU.add))

        with nc.Block() as block:
            @block.vector
            def _(vector):
                emit_butterflies(nc, vector, u1, 16, TPC, sev, last_out=u_bf)
                sev.inc(vector.tensor_copy(id_bf[0:1, 0:8], id_bf[0:1, 0:8]))

    u_ready = sev.n
    w_ready = swd.n
    with ExitStack() as es3:
        h1 = es3.enter_context(nc.sbuf_tensor("h1", [128, 64 * 256], BF16))
        v_bf = es3.enter_context(nc.sbuf_tensor("v_bf", [128, 64 * 256], BF16))
        wub = [es3.enter_context(nc.sbuf_tensor(f"wub{i}", [128, 4096], BF16))
               for i in range(2)]
        wgb = [es3.enter_context(nc.sbuf_tensor(f"wgb{i}", [128, 4096], BF16))
               for i in range(2)]
        sg_t = [es3.enter_context(nc.sbuf_tensor(f"sg_t{i}", [128, 256], BF16))
                for i in range(2)]
        sldU = Sem(es, nc, "sldU")
        sldG = Sem(es, nc, "sldG")
        lv = [es3.enter_context(nc.sbuf_tensor(f"lv{i}", [128, 128], BF16))
              for i in range(2)]
        wdt = [es3.enter_context(nc.sbuf_tensor(f"wdt{i}", [128, 2 * 1024], BF16))
               for i in range(2)]
        F_t = [es3.enter_context(nc.sbuf_tensor(f"F{i}", [128, D], F32))
               for i in range(2)]
        xq = es3.enter_context(nc.sbuf_tensor("xq", [128, D], F32))
        o_sb = es3.enter_context(nc.sbuf_tensor("co_sb", [128, D], F32))
        ln_sc3 = wub[0][:, 0:D]
        ln_stats3 = es3.enter_context(nc.sbuf_tensor("ln_stats3", [128, 8], F32))

        for q in range(4):
            with ExitStack() as esg:
                psA = [esg.enter_context(nc.psum_tensor(f"psA{q}{i}", [128, 512], F32))
                       for i in range(4)]
                psB = [esg.enter_context(nc.psum_tensor(f"psB{q}{i}", [128, 512], F32))
                       for i in range(4)]
                bsm0 = smm.n
                bev0 = sev.n
                bss0 = sss.n
                bldU = sldU.n
                bldG = sldG.n
                with nc.Block() as block:
                    @block.sync
                    def _(sync, q=q, bsm0=bsm0, bldU=bldU, bldG=bldG):
                        for hj in range(32):
                            fc, h2 = hj // 2, hj % 2
                            if q == 0 and hj == 0:
                                sync.wait_ge(swd.h, swd.n)
                            if hj == 0:
                                sync.wait_ge(lns.h, lns.n)
                                if q > 0:
                                    sync.wait_ge(smm.h, bsm0)
                            if hj >= 2:
                                fcp = (hj - 2) // 2
                                sync.wait_ge(smm.h, bsm0 + 2 * fcp + 1)
                            sldU.inc(sync.dma_start(
                                out=wub[hj % 2][:],
                                in_=ubf_d.rearrange("(c p) f -> p c f", p=128)[
                                    :, 8 * h2:8 * (h2 + 1),
                                    512 * fc:512 * (fc + 1)]), 16)
                            if hj >= 2:
                                fcp = (hj - 2) // 2
                                sync.wait_ge(smm.h, bsm0 + 2 * fcp + 2)
                            sldG.inc(sync.dma_start(
                                out=wgb[hj % 2][:],
                                in_=gbf_d.rearrange("(c p) f -> p c f", p=128)[
                                    :, 8 * h2:8 * (h2 + 1),
                                    512 * fc:512 * (fc + 1)]), 16)

                    @block.tensor
                    def _(tensor, q=q, bsm0=bsm0, bev0=bev0, bss0=bss0,
                          bldU=bldU, bldG=bldG):
                        if q == 0:
                            tensor.wait_ge(sev.h, u_ready)
                        for fc in range(16):
                            if fc >= 1:
                                tensor.wait_ge(sev.h, bev0 + 8 * fc)
                                tensor.wait_ge(sss.h, bss0 + 4 * fc)
                            for sub in range(4):
                                for c in range(16):
                                    hj = 2 * fc + c // 8
                                    if sub == 0 and c % 8 == 0:
                                        tensor.wait_ge(sldU.h, bldU + 16 * (hj + 1))
                                    mm = tensor.matmul(
                                        psA[sub][:, 0:256],
                                        wub[hj % 2][:, 512 * (c % 8) + 128 * sub:
                                                     512 * (c % 8) + 128 * (sub + 1)],
                                        u_bf[:, TPC * c + 256 * q:
                                                TPC * c + 256 * (q + 1)],
                                        start=(c == 0), stop=(c == 15))
                            smm.inc(mm)
                            for sub in range(4):
                                for c in range(16):
                                    hj = 2 * fc + c // 8
                                    if sub == 0 and c % 8 == 0:
                                        tensor.wait_ge(sldG.h, bldG + 16 * (hj + 1))
                                    mm = tensor.matmul(
                                        psB[sub][:, 0:256],
                                        wgb[hj % 2][:, 512 * (c % 8) + 128 * sub:
                                                     512 * (c % 8) + 128 * (sub + 1)],
                                        u_bf[:, TPC * c + 256 * q:
                                                TPC * c + 256 * (q + 1)],
                                        start=(c == 0), stop=(c == 15))
                            smm.inc(mm)

                    @block.scalar
                    def _(scalar, q=q, bsm0=bsm0, bev0=bev0, bss0=bss0):
                        for fc in range(16):
                            for sub in range(4):
                                j = 4 * fc + sub
                                scalar.wait_ge(smm.h, bsm0 + 2 * fc + 2)
                                if j >= 2:
                                    scalar.wait_ge(sev.h, bev0 + 2 * (j - 1))
                                sss.inc(scalar.activation(
                                    sg_t[sub % 2][:], psB[sub][:, 0:256], AF.Silu,
                                    scale=wsg[:, j:j + 1]))

                    @block.vector
                    def _(vector, q=q, bsm0=bsm0, bev0=bev0, bss0=bss0):
                        for fc in range(16):
                            for sub in range(4):
                                j = 4 * fc + sub
                                vector.wait_ge(smm.h, bsm0 + 2 * fc + 2)
                                vector.wait_ge(sss.h, bss0 + j + 1)
                                sev.inc(vector.tensor_scalar(
                                    h1[:, 256 * j:256 * (j + 1)], psA[sub][:, 0:256],
                                    wsu[:, j:j + 1], su_dp[:, j:j + 1],
                                    op0=ALU.mult, op1=ALU.mult))
                                vector.wait_ge(sev.h, sev.n)
                                sev.inc(vector.tensor_tensor(
                                    h1[:, 256 * j:256 * (j + 1)],
                                    h1[:, 256 * j:256 * (j + 1)],
                                    sg_t[sub % 2][:], op=ALU.mult))
                                vector.wait_ge(sev.h, sev.n)

            with nc.Block() as block:
                @block.vector
                def _(vector, q=q):
                    emit_butterflies(nc, vector, h1, 64, 256, sev, last_out=v_bf)
                    sev.inc(vector.tensor_copy(id_bf[0:1, 0:8], id_bf[0:1, 0:8]))

            with ExitStack() as esd:
                psD = [[esd.enter_context(
                    nc.psum_tensor(f"psD{q}{tk}{f2}", [128, 512], F32))
                    for f2 in range(2)] for tk in range(2)]
                psH = [esd.enter_context(nc.psum_tensor(f"psH{q}{i}", [128, 512], F32))
                       for i in range(2)]
                for pas in range(2):
                    bsm = smm.n
                    bev = sev.n
                    with nc.Block() as block:
                        @block.sync
                        def _(sync, pas=pas, q=q, bsm=bsm):
                            for g8 in range(32):
                                sync.wait_ge(smm.h, bsm + max(0, (g8 - 1) * 2 - 1) * 6)
                                sd.inc(sync.dma_start(
                                    out=wdt[g8 % 2][:],
                                    in_=dbf_d.rearrange("(g p) f -> p g f", p=128)[
                                        :, 2 * g8:2 * (g8 + 1),
                                        1024 * pas:1024 * (pas + 1)]), 16)
                                sync.wait_ge(sd.h, sd.n)

                        @block.tensor
                        def _(tensor, pas=pas, q=q, bsm=bsm, bev=bev):
                            tensor.wait_ge(sev.h, bev)
                            bsd = sd.n - 32 * 16
                            for c in range(64):
                                tensor.wait_ge(sd.h, bsd + (c // 2 + 1) * 16)
                                for tk in range(2):
                                    if c >= 1:
                                        tensor.wait_ge(sev.h, bev + 2 * (c - 1) + tk + 1)
                                    smm.inc(tensor.matmul(
                                        psH[tk][:, 0:128], H_bf[:],
                                        v_bf[:, 256 * c + 128 * tk:
                                                256 * c + 128 * (tk + 1)],
                                        start=True, stop=True))
                                for tk in range(2):
                                    tensor.wait_ge(sev.h, bev + 2 * c + tk + 1)
                                    for f2 in range(2):
                                        smm.inc(tensor.matmul(
                                            psD[tk][f2][:], lv[tk][:],
                                            wdt[(c // 2) % 2][:, 1024 * (c % 2) +
                                                              512 * f2:
                                                              1024 * (c % 2) +
                                                              512 * (f2 + 1)],
                                            start=(c == 0), stop=(c == 63)))

                        @block.vector
                        def _(vector, pas=pas, q=q, bsm=bsm, bev=bev):
                            for c in range(64):
                                for tk in range(2):
                                    vector.wait_ge(smm.h, bsm + 6 * c + tk + 1)
                                    sev.inc(vector.tensor_copy(
                                        lv[tk][:], psH[tk][:, 0:128]))

                    bsm2 = smm.n
                    with nc.Block() as block:
                        @block.vector
                        def _(vector, pas=pas, q=q, bsm2=bsm2):
                            vector.wait_ge(smm.h, bsm2)
                            vector.wait_ge(lns.h, lns.n)
                            for tk in range(2):
                                for f2 in range(2):
                                    col = 1024 * pas + 512 * f2
                                    sev.inc(vector.tensor_tensor(
                                        F_t[tk][:, col:col + 512],
                                        psD[tk][f2][:],
                                        wdp_bc[:, col:col + 512], op=ALU.mult))

            for tk in range(2):
                tt = 2 * q + tk
                with nc.Block() as block:
                    @block.sync
                    def _(sync, tt=tt):
                        sync.wait_ge(sev.h, sev.n)
                        sd.inc(sync.dma_start(out=xq[:],
                                              in_=x1[128 * tt:128 * (tt + 1), :]), 16)
                        sd.wait(sync)

                with nc.Block() as block:
                    @block.vector
                    def _(vector):
                        vector.wait_ge(sst.h, sst.n)
                emit_ln_tiles(nc, [F_t[tk]], [o_sb], 1, D, es3, f"cf{tt}",
                              shared=(ln_sc3, ln_stats3), sems=(lnv, lns),
                              pre_waits=[(sev, sev.n)])
                with nc.Block() as block:
                    @block.vector
                    def _(vector, tt=tt, tk=tk):
                        vector.wait_ge(lnv.h, lnv.n)
                        sev.inc(vector.tensor_tensor(o_sb[:], o_sb[:], xq[:],
                                                     op=ALU.add))

                with nc.Block() as block:
                    @block.sync
                    def _(sync, tt=tt):
                        sync.wait_ge(sev.h, sev.n)
                        sst.inc(sync.dma_start(out=x2[128 * tt:128 * (tt + 1), :],
                                               in_=o_sb[:]), 16)
                        sync.wait_ge(sst.h, sst.n)

        with nc.Block() as block:
            @block.sync
            def _(sync):
                sst.wait(sync)
    es.close()
    return nc



def build_l1qkv(ncp=None, io=None):
    nc = ncp if ncp is not None else bass.Bass("TRN2", target_bir_lowering=False)
    IN, OUT = _io_funcs(nc, io)
    x2 = IN("x2", [TPC, D], F32)
    WqT = IN("WqT", [D, D], F32)
    WkT = IN("WkT", [D, 512], F32)
    WvT = IN("WvT", [D, 512], F32)
    cosq = IN("cosq", [TPC, D], F32)
    sinq = IN("sinq", [TPC, D], F32)
    identf = IN("identf", [128, 128], F32)
    qro = OUT("qro", [TPC, D], F32)
    qfm = OUT("qfm", [D, TPC], F32)
    kbuf = OUT("kbuf", [512, TPC], F32)
    vbuf = OUT("vbuf", [TPC, 520], F32)
    qmax2 = OUT("qmax2", [32, 1], F32)
    kmax2 = OUT("kmax2", [8, 1], F32)

    es = ExitStack()
    sd = Sem(es, nc, "sd")
    sst = Sem(es, nc, "sst")
    lnv = Sem(es, nc, "lnv")
    lns = Sem(es, nc, "lns")
    trt = Sem(es, nc, "trt")
    trv = Sem(es, nc, "trv")
    scv = Sem(es, nc, "scv")
    smm = Sem(es, nc, "smm")
    sev = Sem(es, nc, "sev")
    sss = Sem(es, nc, "sss")

    id_f = es.enter_context(nc.sbuf_tensor("id_f", [128, 128], F32))
    id_bf = es.enter_context(nc.sbuf_tensor("id_bf", [128, 128], BF16))
    tT = es.enter_context(nc.sbuf_tensor("tT", [128, 16 * TPC], BF16))
    kfm = es.enter_context(nc.sbuf_tensor("kfm", [128, 4 * TPC], F32))
    wq = [es.enter_context(nc.sbuf_tensor(f"wq{i}", [128, D], BF16)) for i in range(16)]
    wk = [es.enter_context(nc.sbuf_tensor(f"wk{i}", [128, 512], BF16)) for i in range(16)]
    wv = [es.enter_context(nc.sbuf_tensor(f"wv{i}", [128, 512], BF16)) for i in range(16)]
    qmx = es.enter_context(nc.sbuf_tensor("qmx", [32, 2], F32))
    kmx = es.enter_context(nc.sbuf_tensor("kmx", [8, 2], F32))
    psT2 = [es.enter_context(nc.psum_tensor(f"psT{i}", [128, 1024], BF16))
            for i in range(2)]
    psq = [es.enter_context(nc.psum_tensor(f"psq{i}", [128, 512], F32))
           for i in range(2)]
    psm = es.enter_context(nc.psum_tensor("psm", [128, 128], F32))

    with nc.Block() as block:
        @block.sync
        def _(sync):
            sd.inc(sync.dma_start(out=id_f[:], in_=identf[:, :]), 16)
            sd.wait(sync)

    with nc.Block() as block:
        @block.vector
        def _(vector):
            vector.memset(qmx[:], 0.0)
            vector.memset(kmx[:], 0.0)
            scv.inc(vector.tensor_copy(id_bf[:], id_f[:]))

    with ExitStack() as esw:
        wst = [esw.enter_context(nc.sbuf_tensor(f"dwst{i}", [128, D], F32))
               for i in range(2)]
        jobs = [("q", i) for i in range(16)] + [("k", i) for i in range(16)] + \
               [("v", i) for i in range(16)]
        for j, (kind, i) in enumerate(jobs):
            src = {"q": WqT, "k": WkT, "v": WvT}[kind]
            w = D if kind == "q" else 512
            dst = {"q": wq, "k": wk, "v": wv}[kind][i]
            with nc.Block() as block:
                @block.sync
                def _(sync, j=j, src=src, i=i, w=w):
                    if j >= 2:
                        sync.wait_ge(scv.h, scv.n)
                    sd.inc(sync.dma_start(out=wst[j % 2][:, 0:w],
                                          in_=src[128 * i:128 * (i + 1), 0:w]), 16)
                    sd.wait(sync)

            with nc.Block() as block:
                @block.vector
                def _(vector, j=j, dst=dst, w=w):
                    scv.inc(vector.tensor_copy(dst[:, 0:w], wst[j % 2][:, 0:w]))

    with ExitStack() as es1:
        x_sb = [es1.enter_context(nc.sbuf_tensor(f"dx{i}", [128, D], F32))
                for i in range(2)]
        t_bf = [es1.enter_context(nc.sbuf_tensor(f"dt{i}", [128, D], BF16))
                for i in range(2)]
        ln_sc = es1.enter_context(nc.sbuf_tensor("ln_sc", [128, D], BF16))
        ln_stats = es1.enter_context(nc.sbuf_tensor("ln_stats", [128, 8], F32))
        for i in range(NTILE):
            with nc.Block() as block:
                @block.sync
                def _(sync, i=i):
                    sd.inc(sync.dma_start(out=x_sb[i % 2][:],
                                          in_=x2[128 * i:128 * (i + 1), :]), 16)
                    sd.wait(sync)

            emit_ln_tiles(nc, [x_sb[i % 2]], [t_bf[i % 2]], 1, D, es1, f"d{i}",
                          shared=(ln_sc, ln_stats), sems=(lnv, lns))
            emit_transpose_to_fm(nc, (trt, trv), [t_bf[i % 2]], tT, 1, 16,
                                 psT2, id_bf, TPC, dst_tile_idx=i,
                                 pre_waits=[(lnv, lnv.n)])

    with ExitStack() as es2:
        q_tm = es2.enter_context(nc.sbuf_tensor("q_tm", [128, D], F32))
        q_ro = es2.enter_context(nc.sbuf_tensor("q_ro", [128, D], F32))
        kv_tm = es2.enter_context(nc.sbuf_tensor("kv_tm", [128, 1024], F32))
        k_ro = es2.enter_context(nc.sbuf_tensor("k_ro", [128, 512], F32))
        vb_sb = es2.enter_context(nc.sbuf_tensor("vb_sb", [128, 520], F32))
        qf_sb = [es2.enter_context(nc.sbuf_tensor(f"qf{i}", [128, 128], F32))
                 for i in range(2)]
        cos_t = es2.enter_context(nc.sbuf_tensor("cos_t", [128, D], F32))
        sin_t = es2.enter_context(nc.sbuf_tensor("sin_t", [128, D], F32))
        rsc = es2.enter_context(nc.sbuf_tensor("rsc", [128, 2048], F32))
        sq_bf = es2.enter_context(nc.sbuf_tensor("sq_bf", [128, D], BF16))
        seg = es2.enter_context(nc.sbuf_tensor("seg", [128, 40], F32))

        with nc.Block() as block:
            @block.vector
            def _(vector):
                vector.memset(vb_sb[:], 1.0)

        def rope_ops(vector, dst, src, width):
            se = src.rearrange("p (x two) -> p two x", two=2)[:, 0]
            so = src.rearrange("p (x two) -> p two x", two=2)[:, 1]
            de = dst.rearrange("p (x two) -> p two x", two=2)[:, 0]
            do = dst.rearrange("p (x two) -> p two x", two=2)[:, 1]
            ce = cos_t[:, 0:width].rearrange("p (x two) -> p two x", two=2)[:, 0]
            sn = sin_t[:, 0:width].rearrange("p (x two) -> p two x", two=2)[:, 0]
            h = width // 2
            sev.inc(vector.tensor_tensor(rsc[:, 0:h], se, ce, op=ALU.mult))
            sev.inc(vector.tensor_tensor(rsc[:, 1024:1024 + h], so, sn, op=ALU.mult))
            vector.wait_ge(sev.h, sev.n)
            sev.inc(vector.tensor_tensor(de, rsc[:, 0:h], rsc[:, 1024:1024 + h],
                                         op=ALU.subtract))
            vector.wait_ge(sev.h, sev.n)
            sev.inc(vector.tensor_tensor(rsc[:, 0:h], se, sn, op=ALU.mult))
            sev.inc(vector.tensor_tensor(rsc[:, 1024:1024 + h], so, ce, op=ALU.mult))
            vector.wait_ge(sev.h, sev.n)
            sev.inc(vector.tensor_tensor(do, rsc[:, 0:h], rsc[:, 1024:1024 + h],
                                         op=ALU.add))
            vector.wait_ge(sev.h, sev.n)

        for tt in range(NTILE):
            with nc.Block() as block:
                @block.sync
                def _(sync, tt=tt):
                    sync.wait_ge(sev.h, sev.n)
                    sd.inc(sync.dma_start(out=cos_t[:],
                                          in_=cosq[128 * tt:128 * (tt + 1), :]), 16)
                    sd.inc(sync.dma_start(out=sin_t[:],
                                          in_=sinq[128 * tt:128 * (tt + 1), :]), 16)
                    sd.wait(sync)

            bmm = smm.n
            bevq = sev.n
            with nc.Block() as block:
                @block.tensor
                def _(tensor, tt=tt, bmm=bmm, bevq=bevq):
                    if tt == 0:
                        tensor.wait_ge(trv.h, trv.n)
                        tensor.wait_ge(scv.h, scv.n)
                    for j in range(6):
                        if j >= 2:
                            tensor.wait_ge(sev.h, bevq + j - 1)
                        for i in range(16):
                            if j < 4:
                                rhs = wq[i][:, 512 * j:512 * (j + 1)]
                            elif j == 4:
                                rhs = wk[i][:, 0:512]
                            else:
                                rhs = wv[i][:, 0:512]
                            mm = tensor.matmul(
                                psq[j % 2][:],
                                tT[:, TPC * i + 128 * tt:TPC * i + 128 * (tt + 1)],
                                rhs, start=(i == 0), stop=(i == 15))
                        smm.inc(mm)

                @block.vector
                def _(vector, tt=tt, bmm=bmm):
                    for j in range(6):
                        vector.wait_ge(smm.h, bmm + j + 1)
                        if j < 4:
                            sev.inc(vector.tensor_copy(
                                q_tm[:, 512 * j:512 * (j + 1)], psq[j % 2][:]))
                        elif j == 4:
                            sev.inc(vector.tensor_copy(kv_tm[:, 0:512], psq[j % 2][:]))
                        else:
                            sev.inc(vector.tensor_copy(kv_tm[:, 512:1024], psq[j % 2][:]))

            with nc.Block() as block:
                @block.vector
                def _(vector, tt=tt):
                    if tt >= 1:
                        vector.wait_ge(sst.h, sst.n)
                    rope_ops(vector, q_ro[:], q_tm[:], D)
                    rope_ops(vector, k_ro[:], kv_tm[:, 0:512], 512)
                    sev.inc(vector.tensor_copy(
                        vb_sb[:].rearrange("p (h e) -> p h e", e=65)[:, :, 0:64],
                        kv_tm[:, 512:1024].rearrange("p (h e) -> p h e", e=64)))

            with nc.Block() as block:
                @block.scalar
                def _(scalar, tt=tt):
                    scalar.wait_ge(sev.h, sev.n)
                    sss.inc(scalar.activation(sq_bf[:], q_ro[:], AF.Square))

            with nc.Block() as block:
                @block.vector
                def _(vector, tt=tt):
                    vector.wait_ge(sss.h, sss.n)
                    sev.inc(vector.tensor_reduce(
                        seg[:, 0:32], sq_bf[:].rearrange("p (h e) -> p h e", e=64),
                        axis=AX.X, op=ALU.add))

            with nc.Block() as block:
                @block.scalar
                def _(scalar, tt=tt):
                    sss.inc(scalar.activation(sq_bf[:, 0:512], k_ro[:], AF.Square))

            with nc.Block() as block:
                @block.vector
                def _(vector, tt=tt):
                    vector.wait_ge(sss.h, sss.n)
                    sev.inc(vector.tensor_reduce(
                        seg[:, 32:40],
                        sq_bf[:, 0:512].rearrange("p (h e) -> p h e", e=64),
                        axis=AX.X, op=ALU.add))

            with nc.Block() as block:
                @block.tensor
                def _(tensor, tt=tt):
                    tensor.wait_ge(sev.h, sev.n)
                    smm.inc(tensor.transpose(psm[0:40, 0:128], seg[:, 0:40], id_f[:]))

            with nc.Block() as block:
                @block.vector
                def _(vector, tt=tt):
                    vector.wait_ge(smm.h, smm.n)
                    sev.inc(vector.tensor_reduce(qmx[:, 1:2], psm[0:32, 0:128],
                                                 axis=AX.X, op=ALU.max))
                    sev.inc(vector.tensor_reduce(kmx[:, 1:2], psm[32:40, 0:128],
                                                 axis=AX.X, op=ALU.max))
                    vector.wait_ge(sev.h, sev.n)
                    sev.inc(vector.tensor_tensor(qmx[:, 0:1], qmx[:, 0:1],
                                                 qmx[:, 1:2], op=ALU.max))
                    sev.inc(vector.tensor_tensor(kmx[:, 0:1], kmx[:, 0:1],
                                                 kmx[:, 1:2], op=ALU.max))

            bq = smm.n
            bqe = sev.n
            bst = sst.n
            with nc.Block() as block:
                @block.tensor
                def _(tensor, tt=tt, bq=bq, bqe=bqe):
                    tensor.wait_ge(sev.h, sev.n)
                    for f in range(16):
                        if f >= 1:
                            tensor.wait_ge(sev.h, bqe + f)
                        smm.inc(tensor.transpose(psm[:, 0:128],
                                                 q_ro[:, 128 * f:128 * (f + 1)],
                                                 id_f[:]))
                        tensor.wait_ge(smm.h, smm.n)

                @block.vector
                def _(vector, tt=tt, bq=bq, bqe=bqe, bst=bst):
                    for f in range(16):
                        vector.wait_ge(smm.h, bq + f + 1)
                        if f >= 2:
                            vector.wait_ge(sst.h, bst + 16 * (f - 1))
                        sev.inc(vector.tensor_copy(qf_sb[f % 2][:], psm[:, 0:128]))

                @block.sync
                def _(sync, tt=tt, bq=bq, bqe=bqe, bst=bst):
                    for f in range(16):
                        sync.wait_ge(sev.h, bqe + f + 1)
                        sync.wait_ge(sst.h, bst + 16 * f)
                        sst.inc(sync.dma_start(
                            out=qfm[128 * f:128 * (f + 1), 128 * tt:128 * (tt + 1)],
                            in_=qf_sb[f % 2][:]), 16)

            with nc.Block() as block:
                @block.tensor
                def _(tensor, tt=tt):
                    tensor.wait_ge(sev.h, sev.n)
                    for r in range(4):
                        smm.inc(tensor.transpose(psm[:, 0:128],
                                                 k_ro[:, 128 * r:128 * (r + 1)],
                                                 id_f[:]))
                        tensor.wait_ge(smm.h, smm.n)
                        tensor.wait_ge(sev.h, sev.n + r + 1)

                @block.vector
                def _(vector, tt=tt):
                    base = smm.n - 4
                    bv = sev.n
                    for r in range(4):
                        vector.wait_ge(smm.h, base + r + 1)
                        sev.inc(vector.tensor_copy(
                            kfm[:, TPC * r + 128 * tt:TPC * r + 128 * (tt + 1)],
                            psm[:, 0:128]))

            with nc.Block() as block:
                @block.sync
                def _(sync, tt=tt):
                    sync.wait_ge(sev.h, sev.n)
                    sst.inc(sync.dma_start(out=qro[128 * tt:128 * (tt + 1), :],
                                           in_=q_ro[:]), 16)
                    sync.wait_ge(sst.h, sst.n)
                    sst.inc(sync.dma_start(out=vbuf[128 * tt:128 * (tt + 1), :],
                                           in_=vb_sb[:]), 16)
                    sync.wait_ge(sst.h, sst.n)

        with nc.Block() as block:
            @block.sync
            def _(sync):
                sync.wait_ge(sev.h, sev.n)
                for r in range(4):
                    sst.inc(sync.dma_start(out=kbuf[128 * r:128 * (r + 1), :],
                                           in_=kfm[:, TPC * r:TPC * (r + 1)]), 16)
                    sync.wait_ge(sst.h, sst.n)
                sst.inc(sync.dma_start(out=qmax2[:, :], in_=qmx[:, 0:1]), 16)
                sync.wait_ge(sst.h, sst.n)
                sst.inc(sync.dma_start(out=kmax2[:, :], in_=kmx[:, 0:1]), 16)
                sst.wait(sync)
    es.close()
    return nc



def build_l1attn(core, ncp=None, io=None):
    nc = ncp if ncp is not None else bass.Bass("TRN2", target_bir_lowering=False)
    IN, OUT = _io_funcs(nc, io)
    qfm = IN("qfm", [D, TPC], F32)
    Kall = IN("Kall", [512, T], F32)
    Vall = IN("Vall", [T, 520], F32)
    qmax2 = IN("qmax2", [32, 1], F32)
    kmax2g = IN("kmax2g", [8, 8], F32)
    x2 = IN("x2", [TPC, D], F32)
    WoT = IN("WoT", [D, D], F32)
    relmask = IN("relmask", [512, 512], F32)
    x3 = OUT("x3", [TPC, D], F32)

    ext = [(core + 1) * 512, (16 - core) * 512]
    maxk = max(ext)
    nsb = [e // 128 for e in ext]
    nsbmax = maxk // 128

    es = ExitStack()
    sd = Sem(es, nc, "sd")
    sst = Sem(es, nc, "sst")
    lnv = Sem(es, nc, "lnv")
    lns = Sem(es, nc, "lns")
    scv = Sem(es, nc, "scv")
    smm = Sem(es, nc, "smm")
    sev = Sem(es, nc, "sev")
    sT = Sem(es, nc, "a_sT")
    sE = Sem(es, nc, "a_sE")
    sP = Sem(es, nc, "a_sP")
    sO = Sem(es, nc, "a_sO")
    sR = Sem(es, nc, "a_sR")

    O_fm = es.enter_context(nc.sbuf_tensor("O_fm", [128, 16 * TPC], BF16))
    rm_bf = es.enter_context(nc.sbuf_tensor("rm_bf", [128, 4 * 512], BF16))
    Mneg = es.enter_context(nc.sbuf_tensor("Mneg", [128, 32], F32))
    ones1 = es.enter_context(nc.sbuf_tensor("ones1", [1, 128], F32))
    mrow = es.enter_context(nc.sbuf_tensor("mrow", [1, 64], F32))
    mwork = es.enter_context(nc.sbuf_tensor("mwork", [1, 64], F32))
    qm2 = es.enter_context(nc.sbuf_tensor("qm2", [32, 1], F32))
    km2g = es.enter_context(nc.sbuf_tensor("km2g", [8, 8], F32))
    km2 = es.enter_context(nc.sbuf_tensor("km2", [8, 1], F32))
    id_f = es.enter_context(nc.sbuf_tensor("id_f", [128, 128], F32))

    with nc.Block() as block:
        @block.sync
        def _(sync):
            sd.inc(sync.dma_start(out=qm2[:], in_=qmax2[:, :]), 16)
            sd.inc(sync.dma_start(out=km2g[:], in_=kmax2g[:, :]), 16)
            sd.wait(sync)

    with nc.Block() as block:
        @block.vector
        def _(vector):
            vector.memset(ones1[:], 1.0)
            vector.memset(id_f[:], 0.0)
            sev.inc(vector.tensor_reduce(km2[:], km2g[:], axis=AX.X, op=ALU.max))

    with nc.Block() as block:
        @block.scalar
        def _(scalar):
            scalar.wait_ge(sev.h, sev.n)
            scalar.activation(qm2[:], qm2[:], AF.Sqrt)
            lns.inc(scalar.activation(km2[:], km2[:], AF.Sqrt))

    with ExitStack() as esm:
        psm = esm.enter_context(nc.psum_tensor("psm", [128, 512], F32))
        scr = nc.dram_tensor("mscr", [64, 1], F32).ap()
        with nc.Block() as block:
            @block.sync
            def _(sync):
                sync.wait_ge(lns.h, lns.n)
                sd.inc(sync.dma_start(out=scr[0:32, 0:1], in_=qm2[:]), 16)
                sync.wait_ge(sd.h, sd.n)
                sd.inc(sync.dma_start(out=scr[32:40, 0:1], in_=km2[:]), 16)
                sync.wait_ge(sd.h, sd.n)
                sd.inc(sync.dma_start(out=mrow[0:1, 0:40],
                                      in_=scr.rearrange("a b -> b a")[0:1, 0:40]), 16)
                sd.wait(sync)

        with nc.Block() as block:
            @block.vector
            def _(vector):
                for g in range(4):
                    sev.inc(vector.tensor_copy(
                        mwork[0:1, 0:32].rearrange("p (k g) -> p g k", g=4)[:, g],
                        mrow[0:1, 32:40]))
                vector.wait_ge(sev.h, sev.n)
                sev.inc(vector.tensor_tensor(mwork[0:1, 0:32], mwork[0:1, 0:32],
                                             mrow[0:1, 0:32], op=ALU.mult))
                vector.wait_ge(sev.h, sev.n)
                sev.inc(vector.tensor_scalar(mwork[0:1, 0:32], mwork[0:1, 0:32],
                                             -0.125, 0.0, op0=ALU.mult, op1=ALU.add))

        with nc.Block() as block:
            @block.tensor
            def _(tensor):
                tensor.wait_ge(sev.h, sev.n)
                smm.inc(tensor.matmul(psm[:, 0:32], ones1[0:1, :],
                                      mwork[0:1, 0:32], start=True, stop=True))

            @block.vector
            def _(vector):
                vector.wait_ge(smm.h, smm.n)
                sev.inc(vector.tensor_copy(Mneg[:], psm[:, 0:32]))

    with ExitStack() as esr:
        mst = esr.enter_context(nc.sbuf_tensor("mst", [128, 512], F32))
        for i in range(4):
            with nc.Block() as block:
                @block.sync
                def _(sync, i=i):
                    sync.wait_ge(scv.h, scv.n)
                    sd.inc(sync.dma_start(out=mst[:],
                                          in_=relmask[128 * i:128 * (i + 1), :]), 16)
                    sd.wait(sync)

            with nc.Block() as block:
                @block.vector
                def _(vector, i=i):
                    scv.inc(vector.tensor_copy(rm_bf[:, 512 * i:512 * (i + 1)], mst[:]))

    with ExitStack() as esa:
        K_sb = esa.enter_context(nc.sbuf_tensor("K_sb", [128, 4 * maxk], BF16))
        V_sb = esa.enter_context(nc.sbuf_tensor("V_sb", [128, nsbmax * 520], BF16))
        kst = [esa.enter_context(nc.sbuf_tensor(f"kst{i}", [128, 2048], F32))
               for i in range(2)]
        qst = [esa.enter_context(nc.sbuf_tensor(f"qst{i}", [64, 512], F32))
               for i in range(2)]
        qs_bf = [esa.enter_context(nc.sbuf_tensor(f"qsb{i}", [128, 512], BF16))
                 for i in range(2)]
        p_bf = [esa.enter_context(nc.sbuf_tensor(f"apb{i}", [128, 512], BF16))
                for i in range(2)]
        o_bf = esa.enter_context(nc.sbuf_tensor("o_bf", [64, 512], BF16))
        rec = esa.enter_context(nc.sbuf_tensor("rec", [1, 512], F32))
        psS = [esa.enter_context(nc.psum_tensor(f"psS{i}", [128, 512], F32))
               for i in range(2)]
        psO = esa.enter_context(nc.psum_tensor("psO", [128, 512], F32))
        psB = esa.enter_context(nc.psum_tensor("psB", [128, 512], F32))

        ncol = maxk // 2048 if maxk % 2048 == 0 else maxk // 2048 + 1
        for r in range(4):
            for cc in range(ncol):
                w = min(2048, maxk - 2048 * cc)
                with nc.Block() as block:
                    @block.sync
                    def _(sync, r=r, cc=cc, w=w):
                        sync.wait_ge(scv.h, scv.n)
                        sd.inc(sync.dma_start(
                            out=kst[(r * ncol + cc) % 2][:, 0:w],
                            in_=Kall[128 * r:128 * (r + 1), 2048 * cc:2048 * cc + w]), 16)
                        sync.wait_ge(sd.h, sd.n)

                with nc.Block() as block:
                    @block.vector
                    def _(vector, r=r, cc=cc, w=w):
                        scv.inc(vector.tensor_copy(
                            K_sb[:, maxk * r + 2048 * cc:maxk * r + 2048 * cc + w],
                            kst[(r * ncol + cc) % 2][:, 0:w]))

        for b in range(nsbmax):
            with nc.Block() as block:
                @block.sync
                def _(sync, b=b):
                    sync.wait_ge(scv.h, scv.n)
                    sd.inc(sync.dma_start(out=kst[b % 2][:, 0:520],
                                          in_=Vall[128 * b:128 * (b + 1), :]), 16)
                    sync.wait_ge(sd.h, sd.n)

            with nc.Block() as block:
                @block.vector
                def _(vector, b=b):
                    scv.inc(vector.tensor_copy(V_sb[:, 520 * b:520 * (b + 1)],
                                               kst[b % 2][:, 0:520]))

        for h in range(32):
            hk = h // 4
            for qc in range(2):
                n_kb = nsb[qc]
                with nc.Block() as block:
                    @block.sync
                    def _(sync, h=h, qc=qc):
                        sync.wait_ge(scv.h, scv.n)
                        sd.inc(sync.dma_start(
                            out=qst[(2 * h + qc) % 2][:],
                            in_=qfm[64 * h:64 * (h + 1), 512 * qc:512 * (qc + 1)]), 16)
                        sync.wait_ge(sd.h, sd.n)

                qb = 64 * ((h // 4) % 2)
                with nc.Block() as block:
                    @block.vector
                    def _(vector, h=h, qc=qc, qb=qb):
                        scv.inc(vector.tensor_scalar(
                            qs_bf[(2 * h + qc) % 2][qb:qb + 64, :],
                            qst[(2 * h + qc) % 2][:],
                            0.125, 0.0, op0=ALU.mult, op1=ALU.add))

                bT, bE, bP, bO = sT.n, sE.n, sP.n, sO.n
                with nc.Block() as block:
                    @block.tensor
                    def _(tensor, h=h, qc=qc, hk=hk, n_kb=n_kb,
                          bT=bT, bE=bE, bP=bP, bO=bO):
                        tensor.wait_ge(scv.h, scv.n)
                        for kb in range(n_kb):
                            pp = kb % 2
                            if kb >= 2:
                                tensor.wait_ge(sE.h, bE + kb - 1)
                            if kb == 0:
                                tensor.wait_ge(sO.h, bO)
                            qb = 64 * (hk % 2)
                            smm_ = tensor.matmul(
                                psS[pp][:],
                                K_sb[qb:qb + 64,
                                     maxk * (hk // 2) + 128 * kb:
                                     maxk * (hk // 2) + 128 * (kb + 1)],
                                qs_bf[(2 * h + qc) % 2][qb:qb + 64, :],
                                start=True, stop=True)
                            sT.inc(smm_)
                            tensor.wait_ge(sP.h, bP + kb + 1)
                            sT.inc(tensor.matmul(
                                psO[0:65, 0:512],
                                V_sb[:, 520 * kb + 65 * hk:520 * kb + 65 * (hk + 1)],
                                p_bf[pp][:], start=(kb == 0), stop=(kb == n_kb - 1)))

                    @block.scalar
                    def _(scalar, h=h, qc=qc, n_kb=n_kb, bT=bT, bE=bE):
                        for kb in range(n_kb):
                            pp = kb % 2
                            scalar.wait_ge(sT.h, bT + 2 * kb + 1)
                            sE.inc(scalar.activation(p_bf[pp][:], psS[pp][:], AF.Exp,
                                                     bias=Mneg[:, h:h + 1]))

                    @block.vector
                    def _(vector, h=h, qc=qc, n_kb=n_kb, bT=bT, bE=bE, bP=bP):
                        for kb in range(n_kb):
                            pp = kb % 2
                            vector.wait_ge(sE.h, bE + kb + 1)
                            if kb >= n_kb - 4:
                                m = kb - (n_kb - 4)
                                sP.inc(vector.tensor_tensor(
                                    p_bf[pp][:], p_bf[pp][:],
                                    rm_bf[:, 512 * m:512 * (m + 1)], op=ALU.mult))
                            else:
                                sP.inc(vector.tensor_copy(mwork[0:1, 0:1],
                                                          ones1[0:1, 0:1]))

                with nc.Block() as block:
                    @block.vector
                    def _(vector, h=h, qc=qc):
                        vector.wait_ge(sT.h, sT.n)
                        sR.inc(vector.reciprocal(rec[:], psO[64:65, 0:512]))
                        vector.wait_ge(sR.h, sR.n)
                        sR.inc(vector.tensor_copy(o_bf[:], psO[0:64, 0:512]))

                with nc.Block() as block:
                    @block.tensor
                    def _(tensor, h=h, qc=qc):
                        tensor.wait_ge(sR.h, sR.n)
                        smm.inc(tensor.matmul(psB[0:64, 0:512], ones1[0:1, 0:64],
                                              rec[0:1, :], start=True, stop=True))

                    @block.vector
                    def _(vector, h=h, qc=qc):
                        vector.wait_ge(smm.h, smm.n)
                        ch = h // 2
                        ro = 64 * (h % 2)
                        sO.inc(vector.tensor_tensor(
                            O_fm[ro:ro + 64,
                                 TPC * ch + 512 * qc:TPC * ch + 512 * (qc + 1)],
                            o_bf[:], psB[0:64, 0:512], op=ALU.mult))

    with ExitStack() as es2:
        wo = [es2.enter_context(nc.sbuf_tensor(f"wo{i}", [128, D], BF16))
              for i in range(16)]
        wst = [es2.enter_context(nc.sbuf_tensor(f"ewst{i}", [128, D], F32))
               for i in range(2)]
        psW = [es2.enter_context(nc.psum_tensor(f"psW{i}", [128, 512], F32))
               for i in range(2)]
        y_sb = es2.enter_context(nc.sbuf_tensor("y_sb", [128, D], F32))
        lny = es2.enter_context(nc.sbuf_tensor("lny", [128, D], F32))
        xt = es2.enter_context(nc.sbuf_tensor("ext", [128, D], F32))
        o_sb = es2.enter_context(nc.sbuf_tensor("eo_sb", [128, D], F32))
        ln_sc = es2.enter_context(nc.sbuf_tensor("ln_sc", [128, D], BF16))
        ln_stats = es2.enter_context(nc.sbuf_tensor("ln_stats", [128, 8], F32))
        for i in range(16):
            with nc.Block() as block:
                @block.sync
                def _(sync, i=i):
                    if i >= 2:
                        sync.wait_ge(scv.h, scv.n)
                    sd.inc(sync.dma_start(out=wst[i % 2][:],
                                          in_=WoT[128 * i:128 * (i + 1), :]), 16)
                    sync.wait_ge(sd.h, sd.n)

            with nc.Block() as block:
                @block.vector
                def _(vector, i=i):
                    scv.inc(vector.tensor_copy(wo[i][:], wst[i % 2][:]))

        for tt in range(NTILE):
            with nc.Block() as block:
                @block.sync
                def _(sync, tt=tt):
                    sync.wait_ge(sev.h, sev.n)
                    sd.inc(sync.dma_start(out=xt[:],
                                          in_=x2[128 * tt:128 * (tt + 1), :]), 16)
                    sd.wait(sync)

            bmm = smm.n
            bev = sev.n
            with nc.Block() as block:
                @block.tensor
                def _(tensor, tt=tt, bmm=bmm, bev=bev):
                    if tt == 0:
                        tensor.wait_ge(scv.h, scv.n)
                        tensor.wait_ge(sO.h, sO.n)
                    for fo in range(4):
                        if fo >= 2:
                            tensor.wait_ge(sev.h, bev + fo - 1)
                        for i in range(16):
                            mm = tensor.matmul(
                                psW[fo % 2][:],
                                O_fm[:, TPC * i + 128 * tt:TPC * i + 128 * (tt + 1)],
                                wo[i][:, 512 * fo:512 * (fo + 1)],
                                start=(i == 0), stop=(i == 15))
                        smm.inc(mm)

                @block.vector
                def _(vector, tt=tt, bmm=bmm, bev=bev):
                    for fo in range(4):
                        vector.wait_ge(smm.h, bmm + fo + 1)
                        sev.inc(vector.tensor_copy(y_sb[:, 512 * fo:512 * (fo + 1)],
                                                   psW[fo % 2][:]))

            emit_ln_tiles(nc, [y_sb], [lny], 1, D, es2, f"e{tt}",
                          shared=(ln_sc, ln_stats), sems=(lnv, lns),
                          pre_waits=[(sev, sev.n)])
            with nc.Block() as block:
                @block.vector
                def _(vector, tt=tt):
                    vector.wait_ge(lnv.h, lnv.n)
                    vector.wait_ge(sst.h, sst.n)
                    sev.inc(vector.tensor_tensor(o_sb[:], xt[:], lny[:], op=ALU.add))

            with nc.Block() as block:
                @block.sync
                def _(sync, tt=tt):
                    sync.wait_ge(sev.h, sev.n)
                    sst.inc(sync.dma_start(out=x3[128 * tt:128 * (tt + 1), :],
                                           in_=o_sb[:]), 16)
                    sync.wait_ge(sst.h, sst.n)

        with nc.Block() as block:
            @block.sync
            def _(sync):
                sst.wait(sync)
    es.close()
    return nc



INTER = 5504
NCH = INTER // 128


def build_ghost(ncp=None, io=None, premat=False):
    nc = ncp if ncp is not None else bass.Bass("TRN2", target_bir_lowering=False)
    IN, OUT = _io_funcs(nc, io)
    x3 = IN("x3", [TPC, D], F32)
    lut = IN("lut", [1, 16], F32)
    identf = IN("identf", [128, 128], F32)
    out = OUT("out", [TPC, D], F32)
    if premat:
        Wg_d = IN("Wg_b", [D, INTER], BF16)
        Wu_d = IN("Wu_b", [D, INTER], BF16)
        Wd_d = IN("Wd_b", [INTER, D], BF16)
    else:
        gidxT = IN("gidxT", [D, INTER], I32)
        uidxT = IN("uidxT", [D, INTER], I32)
        didxT = IN("didxT", [INTER, D], I32)
        Wg_d = nc.dram_tensor("Wg_d", [D, INTER], BF16).ap()
        Wu_d = nc.dram_tensor("Wu_d", [D, INTER], BF16).ap()
        Wd_d = nc.dram_tensor("Wd_d", [INTER, D], BF16).ap()

    es = ExitStack()
    sd = Sem(es, nc, "sd")
    sst = Sem(es, nc, "sst")
    lnv = Sem(es, nc, "lnv")
    lns = Sem(es, nc, "lns")
    trt = Sem(es, nc, "trt")
    trv = Sem(es, nc, "trv")
    scv = Sem(es, nc, "scv")
    smm = Sem(es, nc, "smm")
    sev = Sem(es, nc, "sev")
    sss = Sem(es, nc, "sss")
    swd = Sem(es, nc, "swd")

    id_f = es.enter_context(nc.sbuf_tensor("id_f", [128, 128], F32))
    id_bf = es.enter_context(nc.sbuf_tensor("id_bf", [128, 128], BF16))
    lut_sb = es.enter_context(nc.sbuf_tensor("lut_sb", [1, 16], F32))
    lut_bc = es.enter_context(nc.sbuf_tensor("lut_bc", [128, 16], F32))
    ones1 = es.enter_context(nc.sbuf_tensor("ones1", [1, 128], F32))

    with nc.Block() as block:
        @block.sync
        def _(sync):
            sd.inc(sync.dma_start(out=id_f[:], in_=identf[:, :]), 16)
            sync.wait_ge(sd.h, sd.n)
            sd.inc(sync.dma_start(out=lut_sb[:], in_=lut[:, :]), 16)
            sd.wait(sync)

    with nc.Block() as block:
        @block.vector
        def _(vector):
            vector.tensor_copy(id_bf[:], id_f[:])
            scv.inc(vector.memset(ones1[:], 1.0))

    with ExitStack() as es0:
        psb = es0.enter_context(nc.psum_tensor("psb", [128, 512], F32))
        with nc.Block() as block:
            @block.tensor
            def _(tensor):
                tensor.wait_ge(scv.h, scv.n)
                smm.inc(tensor.matmul(psb[:, 0:16], ones1[0:1, :], lut_sb[0:1, :],
                                      start=True, stop=True))

            @block.vector
            def _(vector):
                vector.wait_ge(smm.h, smm.n)
                sev.inc(vector.tensor_copy(lut_bc[:], psb[:, 0:16]))

    with ExitStack() as es1:
      if not premat:
          ist = es1.enter_context(nc.sbuf_tensor("ist", [128, INTER], I32))
          wtile = es1.enter_context(nc.sbuf_tensor("wtile", [128, INTER], BF16))
          wtmp = es1.enter_context(nc.sbuf_tensor("wtmp", [128, INTER], BF16))
          jobs = [(gidxT, Wg_d, r, INTER) for r in range(16)] + \
                 [(uidxT, Wu_d, r, INTER) for r in range(16)] + \
                 [(didxT, Wd_d, r, D) for r in range(NCH)]
          for j, (srci, dstw, r, w) in enumerate(jobs):
              with nc.Block() as block:
                  @block.sync
                  def _(sync, j=j, srci=srci, r=r, w=w):
                      sync.wait_ge(sev.h, sev.n)
                      sd.inc(sync.dma_start(out=ist[:, 0:w],
                                            in_=srci[128 * r:128 * (r + 1), :]), 16)
                      sync.wait_ge(sd.h, sd.n)

              with nc.Block() as block:
                  @block.vector
                  def _(vector, j=j, w=w):
                      vector.wait_ge(swd.h, swd.n)
                      sev.inc(vector.tensor_scalar(wtile[:, 0:w], ist[:, 0:w],
                                                   0.0, lut_bc[:, 0:1],
                                                   op0=ALU.is_equal, op1=ALU.mult))
                      for v in range(1, 16):
                          sev.inc(vector.tensor_scalar(wtmp[:, 0:w], ist[:, 0:w],
                                                       float(v), lut_bc[:, v:v + 1],
                                                       op0=ALU.is_equal, op1=ALU.mult))
                          vector.wait_ge(sev.h, sev.n)
                          sev.inc(vector.tensor_tensor(wtile[:, 0:w], wtile[:, 0:w],
                                                       wtmp[:, 0:w], op=ALU.add))
                          vector.wait_ge(sev.h, sev.n)

              with nc.Block() as block:
                  @block.sync
                  def _(sync, j=j, dstw=dstw, r=r, w=w):
                      sync.wait_ge(sev.h, sev.n)
                      swd.inc(sync.dma_start(out=dstw[128 * r:128 * (r + 1), :],
                                             in_=wtile[:, 0:w]), 16)
                      sync.wait_ge(swd.h, swd.n)

    tT = es.enter_context(nc.sbuf_tensor("tT", [128, 16 * TPC], BF16))
    with ExitStack() as es2:
        psT2 = [es2.enter_context(nc.psum_tensor(f"psT{i}", [128, 1024], BF16))
                for i in range(2)]
        x_sb = [es2.enter_context(nc.sbuf_tensor(f"gx{i}", [128, D], F32))
                for i in range(2)]
        t_bf = [es2.enter_context(nc.sbuf_tensor(f"gt{i}", [128, D], BF16))
                for i in range(2)]
        ln_sc = es2.enter_context(nc.sbuf_tensor("ln_sc", [128, D], BF16))
        ln_stats = es2.enter_context(nc.sbuf_tensor("ln_stats", [128, 8], F32))
        for i in range(NTILE):
            with nc.Block() as block:
                @block.sync
                def _(sync, i=i):
                    sd.inc(sync.dma_start(out=x_sb[i % 2][:],
                                          in_=x3[128 * i:128 * (i + 1), :]), 16)
                    sd.wait(sync)

            emit_ln_tiles(nc, [x_sb[i % 2]], [t_bf[i % 2]], 1, D, es2, f"g{i}",
                          shared=(ln_sc, ln_stats), sems=(lnv, lns))
            emit_transpose_to_fm(nc, (trt, trv), [t_bf[i % 2]], tT, 1, 16,
                                 psT2, id_bf, TPC, dst_tile_idx=i,
                                 pre_waits=[(lnv, lnv.n)])

    h_fm = es.enter_context(nc.sbuf_tensor("h_fm", [128, NCH * TPC], BF16))
    with ExitStack() as es3:
        wg_t = es3.enter_context(nc.sbuf_tensor("wg_t", [128, 16 * 128], BF16))
        wu_t = es3.enter_context(nc.sbuf_tensor("wu_t", [128, 16 * 128], BF16))
        sg_t = es3.enter_context(nc.sbuf_tensor("sg_t", [128, 512], BF16))
        psA = [es3.enter_context(nc.psum_tensor(f"gA{i}", [128, 512], F32))
               for i in range(2)]
        psB = [es3.enter_context(nc.psum_tensor(f"gB{i}", [128, 512], F32))
               for i in range(2)]
        bev0 = sev.n
        bss0 = sss.n
        bsm0 = smm.n
        for j in range(NCH):
            with nc.Block() as block:
                @block.sync
                def _(sync, j=j):
                    if j == 0:
                        sync.wait_ge(swd.h, swd.n)
                    sync.wait_ge(smm.h, smm.n)
                    sd.inc(sync.dma_start(
                        out=wg_t[:],
                        in_=Wg_d.rearrange("(c p) f -> p c f", p=128)[
                            :, :, 128 * j:128 * (j + 1)]), 16)
                    sync.wait_ge(sd.h, sd.n)
                    sd.inc(sync.dma_start(
                        out=wu_t[:],
                        in_=Wu_d.rearrange("(c p) f -> p c f", p=128)[
                            :, :, 128 * j:128 * (j + 1)]), 16)
                    sync.wait_ge(sd.h, sd.n)

            with nc.Block() as block:
                @block.tensor
                def _(tensor, j=j):
                    if j == 0:
                        tensor.wait_ge(trv.h, trv.n)
                    for tc in range(2):
                        idx = 2 * j + tc
                        if idx >= 2:
                            tensor.wait_ge(sev.h, bev0 + idx - 1)
                        for c in range(16):
                            mma = tensor.matmul(psA[tc % 2][:],
                                                wg_t[:, 128 * c:128 * (c + 1)],
                                                tT[:, TPC * c + 512 * tc:
                                                    TPC * c + 512 * (tc + 1)],
                                                start=(c == 0), stop=(c == 15))
                        smm.inc(mma)
                        for c in range(16):
                            mm = tensor.matmul(psB[tc % 2][:],
                                               wu_t[:, 128 * c:128 * (c + 1)],
                                               tT[:, TPC * c + 512 * tc:
                                                   TPC * c + 512 * (tc + 1)],
                                               start=(c == 0), stop=(c == 15))
                        smm.inc(mm)

                @block.scalar
                def _(scalar, j=j):
                    for tc in range(2):
                        idx = 2 * j + tc
                        scalar.wait_ge(smm.h, bsm0 + 4 * j + 2 * tc + 1)
                        if idx >= 1:
                            scalar.wait_ge(sev.h, bev0 + idx)
                        sss.inc(scalar.activation(sg_t[:], psA[tc % 2][:], AF.Silu))

                @block.vector
                def _(vector, j=j):
                    for tc in range(2):
                        idx = 2 * j + tc
                        vector.wait_ge(smm.h, bsm0 + 4 * j + 2 * tc + 2)
                        vector.wait_ge(sss.h, bss0 + idx + 1)
                        sev.inc(vector.tensor_tensor(
                            h_fm[:, TPC * j + 512 * tc:TPC * j + 512 * (tc + 1)],
                            sg_t[:], psB[tc % 2][:], op=ALU.mult))

    with ExitStack() as es4:
        wd_t = [es4.enter_context(nc.sbuf_tensor(f"wd{i}", [128, 2 * D], BF16))
                for i in range(2)]
        psD = [es4.enter_context(nc.psum_tensor(f"gD{i}", [128, 512], F32))
               for i in range(4)]
        F_sb = es4.enter_context(nc.sbuf_tensor("F_sb", [128, D], F32))
        xq = es4.enter_context(nc.sbuf_tensor("gxq", [128, D], F32))
        o_sb = es4.enter_context(nc.sbuf_tensor("go_sb", [128, D], F32))
        ln_sc4 = es4.enter_context(nc.sbuf_tensor("ln_sc4", [128, D], BF16))
        ln_stats4 = es4.enter_context(nc.sbuf_tensor("ln_stats4", [128, 8], F32))
        for tt in range(NTILE):
            bsm = smm.n
            bsd = sd.n
            with nc.Block() as block:
                @block.sync
                def _(sync, tt=tt, bsm=bsm):
                    ng = (NCH + 1) // 2
                    for g2 in range(ng):
                        nc_ = min(2, NCH - 2 * g2)
                        sync.wait_ge(smm.h, bsm + max(0, 2 * (g2 - 1) - 1))
                        sd.inc(sync.dma_start(
                            out=wd_t[g2 % 2][:, 0:nc_ * D],
                            in_=Wd_d.rearrange("(g p) f -> p g f", p=128)[
                                :, 2 * g2:2 * g2 + nc_, :]), 16)
                        sync.wait_ge(sd.h, sd.n)

                @block.tensor
                def _(tensor, tt=tt, bsm=bsm, bsd=bsd):
                    tensor.wait_ge(sev.h, sev.n)
                    for c in range(NCH):
                        tensor.wait_ge(sd.h, bsd + 16 * (c // 2 + 1))
                        for fo in range(4):
                            mm = tensor.matmul(
                                psD[fo][:],
                                h_fm[:, TPC * c + 128 * tt:TPC * c + 128 * (tt + 1)],
                                wd_t[(c // 2) % 2][:, D * (c % 2) + 512 * fo:
                                                   D * (c % 2) + 512 * (fo + 1)],
                                start=(c == 0), stop=(c == NCH - 1))
                        smm.inc(mm)

                @block.vector
                def _(vector, tt=tt, bsm=bsm):
                    vector.wait_ge(smm.h, bsm + NCH)
                    for fo in range(4):
                        sev.inc(vector.tensor_copy(F_sb[:, 512 * fo:512 * (fo + 1)],
                                                   psD[fo][:]))

            with nc.Block() as block:
                @block.sync
                def _(sync, tt=tt):
                    sync.wait_ge(sev.h, sev.n)
                    sd.inc(sync.dma_start(out=xq[:],
                                          in_=x3[128 * tt:128 * (tt + 1), :]), 16)
                    sd.wait(sync)

            emit_ln_tiles(nc, [F_sb], [o_sb], 1, D, es4, f"gf{tt}",
                          shared=(ln_sc4, ln_stats4), sems=(lnv, lns),
                          pre_waits=[(sev, sev.n)])
            with nc.Block() as block:
                @block.vector
                def _(vector, tt=tt):
                    vector.wait_ge(lnv.h, lnv.n)
                    vector.wait_ge(sst.h, sst.n)
                    sev.inc(vector.tensor_tensor(o_sb[:], o_sb[:], xq[:], op=ALU.add))

            with nc.Block() as block:
                @block.sync
                def _(sync, tt=tt):
                    sync.wait_ge(sev.h, sev.n)
                    sst.inc(sync.dma_start(out=out[128 * tt:128 * (tt + 1), :],
                                           in_=o_sb[:]), 16)
                    sync.wait_ge(sst.h, sst.n)

        with nc.Block() as block:
            @block.sync
            def _(sync):
                sst.wait(sync)
    es.close()
    return nc



_BUILD_CACHE = {}


def _get(name, fn):
    if name not in _BUILD_CACHE:
        _BUILD_CACHE[name] = fn()
    return _BUILD_CACHE[name]


def _consts():
    ident = np.eye(128, dtype=np.float32)
    l0mask = np.zeros((128, 128), np.float32)
    for nl in range(8):
        l0mask[nl * 16:(nl + 1) * 16, nl * 16:(nl + 1) * 16] = 1.0
    H = hadamard128()
    relmask = np.zeros((512, 512), np.float32)
    for i in range(4):
        mk = 128 * i + np.arange(128)[:, None]
        lq = np.arange(512)[None, :]
        relmask[128 * i:128 * (i + 1)] = (mk <= lq).astype(np.float32)
    return ident, l0mask, H, relmask


def kernel(L0_attn_x, L0_attn_w_qkv, L0_attn_w_proj, L0_ffn_SU_ug, L0_ffn_SU_dp,
           L0_ffn_Wscale_ug, L0_ffn_Wscale_dp, L0_ffn_W_up, L0_ffn_W_gate,
           L0_ffn_W_down, L1_attn_Wq, L1_attn_Wk, L1_attn_Wv, L1_attn_Wo,
           L1_ffn_lut, L1_ffn_gate_idx, L1_ffn_up_idx, L1_ffn_down_idx):
    x = np.ascontiguousarray(np.asarray(L0_attn_x, dtype=np.float32)[0])
    ident, l0mask, H, relmask = _consts()
    cores = list(range(NCORES))
    rows_c = [shard_rows(c) for c in cores]

    wqkvT = np.ascontiguousarray(np.asarray(L0_attn_w_qkv).T)
    ncA = _get("A", build_l0attn)
    resA = run_bass_kernel_spmd(ncA, [
        dict(xs=np.ascontiguousarray(x[rows_c[c]]), wqkvT=wqkvT,
             identf=ident, l0mask=l0mask) for c in cores], core_ids=cores)
    obufs = [resA.results[c]["obuf"] for c in cores]

    owner = np.empty(T, np.int64)
    local = np.empty(T, np.int64)
    for b in range(16):
        cc, slot = token_owner(b)
        owner[b * BLK:(b + 1) * BLK] = cc
        local[b * BLK:(b + 1) * BLK] = slot * BLK + np.arange(BLK)
    ob_all = np.stack(obufs)

    def build_P(c):
        r = rows_c[c]
        h = r // 512
        g = r % 512
        n = (g[:, None] * 16 + np.arange(16)[None, :]).reshape(-1)
        hh = np.repeat(h, 16)
        P = ob_all[owner[n], local[n] * 16 + hh, :]
        return np.ascontiguousarray(P.reshape(TPC, D))

    inv = 1.0 / 10000.0 ** (np.arange(32) / 32.0)
    nc2 = _get("L2", build_launch2)
    shared2 = {
        "b_wprojT": np.ascontiguousarray(np.asarray(L0_attn_w_proj).T),
        "identf": ident,
        "c_wupT": np.ascontiguousarray(np.asarray(L0_ffn_W_up).T),
        "c_wgateT": np.ascontiguousarray(np.asarray(L0_ffn_W_gate).T),
        "c_wdownT": np.ascontiguousarray(np.asarray(L0_ffn_W_down).T),
        "c_su_ug": np.ascontiguousarray(np.asarray(L0_ffn_SU_ug).reshape(16, 128).T),
        "c_su_dp": np.ascontiguousarray(np.asarray(L0_ffn_SU_dp).reshape(64, 128).T),
        "c_wsu": np.ascontiguousarray(np.asarray(L0_ffn_Wscale_ug)[:8192].reshape(64, 128).T),
        "c_wsg": np.ascontiguousarray(np.asarray(L0_ffn_Wscale_ug)[8192:].reshape(64, 128).T),
        "c_wdp": np.ascontiguousarray(np.asarray(L0_ffn_Wscale_dp).reshape(1, D)),
        "c_Hf": H,
        "d_WqT": np.ascontiguousarray(np.asarray(L1_attn_Wq).T),
        "d_WkT": np.ascontiguousarray(np.asarray(L1_attn_Wk).T),
        "d_WvT": np.ascontiguousarray(np.asarray(L1_attn_Wv).T),
        "g_lut": np.ascontiguousarray(np.asarray(L1_ffn_lut).reshape(1, 16)),
    }
    idx_flat = np.concatenate([
        np.ascontiguousarray(np.asarray(L1_ffn_gate_idx).T).ravel(),
        np.ascontiguousarray(np.asarray(L1_ffn_up_idx).T).ravel(),
        np.ascontiguousarray(np.asarray(L1_ffn_down_idx).T).ravel()]).astype(np.int32)
    idx_parts = idx_flat.reshape(NCORES, 128, GM_W)
    in_maps2 = []
    for c in cores:
        ang = rows_c[c][:, None].astype(np.float64) * inv[None, :]
        in_maps2.append(dict(
            b_Pm=build_P(c), b_xs=np.ascontiguousarray(x[rows_c[c]]),
            d_cosq=np.tile(np.repeat(np.cos(ang), 2, 1), (1, 32)).astype(np.float32),
            d_sinq=np.tile(np.repeat(np.sin(ang), 2, 1), (1, 32)).astype(np.float32),
            g_gidx=np.ascontiguousarray(idx_parts[c]),
            **shared2))
    res2 = run_bass_kernel_spmd(nc2, in_maps2, core_ids=cores)

    Kall = np.empty((512, T), np.float32)
    Vall = np.empty((T, 520), np.float32)
    for c in cores:
        Kall[:, rows_c[c]] = res2.results[c]["d_kbuf"]
        Vall[rows_c[c], :] = res2.results[c]["d_vbuf"]
    kmax2g = np.ascontiguousarray(np.stack(
        [res2.results[c]["d_kmax2"][:, 0] for c in cores], axis=1).astype(np.float32))
    w_flat = np.concatenate([res2.results[c]["g_wpart"].ravel() for c in cores])
    n_gu = D * INTER
    Wg_b = np.ascontiguousarray(w_flat[:n_gu].reshape(D, INTER))
    Wu_b = np.ascontiguousarray(w_flat[n_gu:2 * n_gu].reshape(D, INTER))
    Wd_b = np.ascontiguousarray(w_flat[2 * n_gu:].reshape(INTER, D))

    nc3 = _get("L3", build_launch3)
    shared3 = {
        "e_Kall": Kall, "e_Vall": Vall, "e_kmax2g": kmax2g,
        "e_WoT": np.ascontiguousarray(np.asarray(L1_attn_Wo).T),
        "e_relmask": relmask, "identf": ident,
        "f_lut": np.ascontiguousarray(np.asarray(L1_ffn_lut).reshape(1, 16)),
        "f_Wg_b": Wg_b, "f_Wu_b": Wu_b, "f_Wd_b": Wd_b,
    }
    in_maps3 = []
    for c in cores:
        pen = np.zeros((128, 128), np.float32)
        for qc in range(2):
            lim = 4 * c if qc == 0 else 4 * (15 - c)
            pen[:, qc * 64 + lim:qc * 64 + 64] = -30000.0
        in_maps3.append(dict(
            e_qfm=np.ascontiguousarray(res2.results[c]["d_qfm"]),
            e_kloc=np.ascontiguousarray(Kall[:, rows_c[c]]),
            e_vloc=np.ascontiguousarray(Vall[rows_c[c], :]),
            e_qmax2=np.ascontiguousarray(res2.results[c]["d_qmax2"]),
            e_x2=np.ascontiguousarray(res2.results[c]["x2_h"]),
            e_pen=pen, **shared3))
    res3 = run_bass_kernel_spmd(nc3, in_maps3, core_ids=cores)

    out = np.empty((1, T, D), np.float32)
    for c in cores:
        out[0, rows_c[c], :] = res3.results[c]["f_out"]
    return out



def build_launch2():
    nc = bass.Bass("TRN2", target_bir_lowering=False)
    x1_d = nc.dram_tensor("x1_h", [TPC, D], F32).ap()
    x2_d = nc.dram_tensor("x2_h", [TPC, D], F32, kind="ExternalOutput").ap()
    ident_shared = nc.dram_tensor("identf", [128, 128], F32, kind="ExternalInput").ap()
    build_l0proj(ncp=NCP(nc, "b_"), io=dict(x1=x1_d, identf=ident_shared))
    build_l0ffn(ncp=NCP(nc, "c_"), io=dict(x1=x1_d, x2=x2_d, identf=ident_shared))
    build_l1qkv(ncp=NCP(nc, "d_"), io=dict(x2=x2_d, identf=ident_shared))
    build_ghostmat(ncp=NCP(nc, "g_"), io=None)
    return nc



KVLIM = 15 * 512
NSBL = KVLIM // 128


def build_l1attn_uni(ncp=None, io=None):
    nc = ncp if ncp is not None else bass.Bass("TRN2", target_bir_lowering=False)
    IN, OUT = _io_funcs(nc, io)
    qfm = IN("qfm", [D, TPC], F32)
    Kall = IN("Kall", [512, T], F32)
    Vall = IN("Vall", [T, 520], F32)
    kloc = IN("kloc", [512, TPC], F32)
    vloc = IN("vloc", [TPC, 520], F32)
    qmax2 = IN("qmax2", [32, 1], F32)
    kmax2g = IN("kmax2g", [8, 8], F32)
    x2 = IN("x2", [TPC, D], F32)
    WoT = IN("WoT", [D, D], F32)
    relmask = IN("relmask", [512, 512], F32)
    pen = IN("pen", [128, 128], F32)
    x3 = OUT("x3", [TPC, D], F32)
    o_d = nc.dram_tensor("o_scr", [D, TPC], BF16).ap()

    es = ExitStack()
    sd = Sem(es, nc, "sd")
    sst = Sem(es, nc, "sst")
    lnv = Sem(es, nc, "lnv")
    lns = Sem(es, nc, "lns")
    scv = Sem(es, nc, "scv")
    smm = Sem(es, nc, "smm")
    sev = Sem(es, nc, "sev")
    sT = Sem(es, nc, "a_sT")
    sE = Sem(es, nc, "a_sE")
    sP = Sem(es, nc, "a_sP")
    sO = Sem(es, nc, "a_sO")
    sR = Sem(es, nc, "a_sR")

    rm_bf = es.enter_context(nc.sbuf_tensor("rm_bf", [128, 4 * 512], BF16))
    Mneg = es.enter_context(nc.sbuf_tensor("Mneg", [128, 32], F32))
    Mbias = es.enter_context(nc.sbuf_tensor("Mbias", [128, 96 * 32], F32))
    pen_sb = es.enter_context(nc.sbuf_tensor("pen_sb", [128, 128], F32))
    ones1 = es.enter_context(nc.sbuf_tensor("ones1", [1, 128], F32))
    mrow = es.enter_context(nc.sbuf_tensor("mrow", [1, 64], F32))
    mwork = es.enter_context(nc.sbuf_tensor("mwork", [1, 64], F32))
    qm2 = es.enter_context(nc.sbuf_tensor("qm2", [32, 1], F32))
    km2g = es.enter_context(nc.sbuf_tensor("km2g", [8, 8], F32))
    km2 = es.enter_context(nc.sbuf_tensor("km2", [8, 1], F32))

    with nc.Block() as block:
        @block.sync
        def _(sync):
            sd.inc(sync.dma_start(out=qm2[:], in_=qmax2[:, :]), 16)
            sync.wait_ge(sd.h, sd.n)
            sd.inc(sync.dma_start(out=km2g[:], in_=kmax2g[:, :]), 16)
            sync.wait_ge(sd.h, sd.n)
            sd.inc(sync.dma_start(out=pen_sb[:], in_=pen[:, :]), 16)
            sd.wait(sync)

    with nc.Block() as block:
        @block.vector
        def _(vector):
            vector.memset(ones1[:], 1.0)
            sev.inc(vector.tensor_reduce(km2[:], km2g[:], axis=AX.X, op=ALU.max))

    with nc.Block() as block:
        @block.scalar
        def _(scalar):
            scalar.wait_ge(sev.h, sev.n)
            scalar.activation(qm2[:], qm2[:], AF.Sqrt)
            lns.inc(scalar.activation(km2[:], km2[:], AF.Sqrt))

    with ExitStack() as esm:
        psm = esm.enter_context(nc.psum_tensor("psm", [128, 512], F32))
        scr = nc.dram_tensor("mscr", [64, 1], F32).ap()
        with nc.Block() as block:
            @block.sync
            def _(sync):
                sync.wait_ge(lns.h, lns.n)
                sd.inc(sync.dma_start(out=scr[0:32, 0:1], in_=qm2[:]), 16)
                sync.wait_ge(sd.h, sd.n)
                sd.inc(sync.dma_start(out=scr[32:40, 0:1], in_=km2[:]), 16)
                sync.wait_ge(sd.h, sd.n)
                sd.inc(sync.dma_start(out=mrow[0:1, 0:40],
                                      in_=scr.rearrange("a b -> b a")[0:1, 0:40]), 16)
                sd.wait(sync)

        with nc.Block() as block:
            @block.vector
            def _(vector):
                for g in range(4):
                    sev.inc(vector.tensor_copy(
                        mwork[0:1, 0:32].rearrange("p (k g) -> p g k", g=4)[:, g],
                        mrow[0:1, 32:40]))
                vector.wait_ge(sev.h, sev.n)
                sev.inc(vector.tensor_tensor(mwork[0:1, 0:32], mwork[0:1, 0:32],
                                             mrow[0:1, 0:32], op=ALU.mult))
                vector.wait_ge(sev.h, sev.n)
                sev.inc(vector.tensor_scalar(mwork[0:1, 0:32], mwork[0:1, 0:32],
                                             -0.125, 0.0, op0=ALU.mult, op1=ALU.add))

        with nc.Block() as block:
            @block.tensor
            def _(tensor):
                tensor.wait_ge(sev.h, sev.n)
                smm.inc(tensor.matmul(psm[:, 0:32], ones1[0:1, :],
                                      mwork[0:1, 0:32], start=True, stop=True))

            @block.vector
            def _(vector):
                vector.wait_ge(smm.h, smm.n)
                sev.inc(vector.tensor_copy(Mneg[:], psm[:, 0:32]))

        with nc.Block() as block:
            @block.vector
            def _(vector):
                vector.wait_ge(sev.h, sev.n)
                for qc in range(2):
                    nk = 32 if qc == 0 else 64
                    for kb in range(nk):
                        s = kb if qc == 0 else 32 + kb
                        pc = qc * 64 + kb
                        sev.inc(vector.tensor_scalar(
                            Mbias[:, 32 * s:32 * (s + 1)], Mneg[:, 0:32],
                            pen_sb[:, pc:pc + 1], 0.0, op0=ALU.add, op1=ALU.add))

    with ExitStack() as esr:
        mst = esr.enter_context(nc.sbuf_tensor("mst", [128, 512], F32))
        for i in range(4):
            with nc.Block() as block:
                @block.sync
                def _(sync, i=i):
                    sync.wait_ge(scv.h, scv.n)
                    sd.inc(sync.dma_start(out=mst[:],
                                          in_=relmask[128 * i:128 * (i + 1), :]), 16)
                    sd.wait(sync)

            with nc.Block() as block:
                @block.vector
                def _(vector, i=i):
                    scv.inc(vector.tensor_copy(rm_bf[:, 512 * i:512 * (i + 1)], mst[:]))

    with ExitStack() as esa:
        K_sb = esa.enter_context(nc.sbuf_tensor("K_sb", [128, 4 * KVLIM], BF16))
        V_sb = esa.enter_context(nc.sbuf_tensor("V_sb", [128, NSBL * 520], BF16))
        klb = esa.enter_context(nc.sbuf_tensor("klb", [128, 4 * TPC], BF16))
        vlb = esa.enter_context(nc.sbuf_tensor("vlb", [128, 8 * 520], BF16))
        kst = [esa.enter_context(nc.sbuf_tensor(f"kst{i}", [128, 2048], F32))
               for i in range(2)]
        qst = [esa.enter_context(nc.sbuf_tensor(f"qst{i}", [64, 512], F32))
               for i in range(2)]
        qs_bf = [esa.enter_context(nc.sbuf_tensor(f"qsb{i}", [128, 512], BF16))
                 for i in range(2)]
        p_bf = [esa.enter_context(nc.sbuf_tensor(f"apb{i}", [128, 512], BF16))
                for i in range(4)]
        o_bf = esa.enter_context(nc.sbuf_tensor("o_bf", [64, 512], BF16))
        o_nm = esa.enter_context(nc.sbuf_tensor("o_nm", [64, 512], BF16))
        rec = esa.enter_context(nc.sbuf_tensor("rec", [1, 512], F32))
        psS = [esa.enter_context(nc.psum_tensor(f"psS{i}", [128, 512], F32))
               for i in range(4)]
        psO = esa.enter_context(nc.psum_tensor("psO", [128, 512], F32))
        psB = esa.enter_context(nc.psum_tensor("psB", [128, 512], F32))

        jobs = []
        for r in range(4):
            for cc in range((KVLIM + 2047) // 2048):
                w = min(2048, KVLIM - 2048 * cc)
                jobs.append(("Kg", r, cc, w))
        for b in range(NSBL):
            jobs.append(("Vg", b, 0, 520))
        for r in range(4):
            jobs.append(("Kl", r, 0, TPC))
        for b in range(8):
            jobs.append(("Vl", b, 0, 520))
        for j, (kind, r, cc, w) in enumerate(jobs):
            with nc.Block() as block:
                @block.sync
                def _(sync, j=j, kind=kind, r=r, cc=cc, w=w):
                    sync.wait_ge(scv.h, scv.n)
                    if kind == "Kg":
                        src = Kall[128 * r:128 * (r + 1), 2048 * cc:2048 * cc + w]
                    elif kind == "Vg":
                        src = Vall[128 * r:128 * (r + 1), 0:w]
                    elif kind == "Kl":
                        src = kloc[128 * r:128 * (r + 1), 0:w]
                    else:
                        src = vloc[128 * r:128 * (r + 1), 0:w]
                    sd.inc(sync.dma_start(out=kst[j % 2][:, 0:w], in_=src), 16)
                    sync.wait_ge(sd.h, sd.n)

            with nc.Block() as block:
                @block.vector
                def _(vector, j=j, kind=kind, r=r, cc=cc, w=w):
                    if kind == "Kg":
                        dst = K_sb[:, KVLIM * r + 2048 * cc:KVLIM * r + 2048 * cc + w]
                    elif kind == "Vg":
                        dst = V_sb[:, 520 * r:520 * r + w]
                    elif kind == "Kl":
                        dst = klb[:, TPC * r:TPC * r + w]
                    else:
                        dst = vlb[:, 520 * r:520 * r + w]
                    scv.inc(vector.tensor_copy(dst, kst[j % 2][:, 0:w]))

        for h in range(32):
            hk = h // 4
            qb = 64 * (hk % 2)
            for qc in range(2):
                n_pad = 32 if qc == 0 else 64
                sbase = 0 if qc == 0 else 32
                with nc.Block() as block:
                    @block.sync
                    def _(sync, h=h, qc=qc):
                        sync.wait_ge(scv.h, scv.n)
                        sd.inc(sync.dma_start(
                            out=qst[(2 * h + qc) % 2][:],
                            in_=qfm[64 * h:64 * (h + 1), 512 * qc:512 * (qc + 1)]), 16)
                        sync.wait_ge(sd.h, sd.n)

                with nc.Block() as block:
                    @block.vector
                    def _(vector, h=h, qc=qc, qb=qb):
                        scv.inc(vector.tensor_scalar(
                            qs_bf[(2 * h + qc) % 2][qb:qb + 64, :],
                            qst[(2 * h + qc) % 2][:],
                            0.125, 0.0, op0=ALU.mult, op1=ALU.add))

                bT, bE, bP, bO = sT.n, sE.n, sP.n, sO.n
                n_tot = n_pad + 4
                with nc.Block() as block:
                    @block.tensor
                    def _(tensor, h=h, qc=qc, hk=hk, qb=qb, n_pad=n_pad,
                          bT=bT, bE=bE, bP=bP, bO=bO, n_tot=n_tot):
                        tensor.wait_ge(scv.h, scv.n)
                        for k in range(n_tot):
                            pp = k % 4
                            if k >= 4:
                                tensor.wait_ge(sE.h, bE + k - 3)
                            if k == 0:
                                tensor.wait_ge(sO.h, bO)
                            if k < n_pad:
                                kb = min(k, NSBL - 1)
                                lhs = K_sb[qb:qb + 64,
                                           KVLIM * (hk // 2) + 128 * kb:
                                           KVLIM * (hk // 2) + 128 * (kb + 1)]
                            else:
                                db = k - n_pad
                                lhs = klb[qb:qb + 64,
                                          TPC * (hk // 2) + 512 * qc + 128 * db:
                                          TPC * (hk // 2) + 512 * qc + 128 * (db + 1)]
                            sT.inc(tensor.matmul(psS[pp][:], lhs,
                                                 qs_bf[(2 * h + qc) % 2][qb:qb + 64, :],
                                                 start=True, stop=True))
                            if k < n_pad:
                                tensor.wait_ge(sE.h, bE + k + 1)
                            else:
                                tensor.wait_ge(sP.h, bP + (k - n_pad) + 1)
                            if k < n_pad:
                                kb = min(k, NSBL - 1)
                                vl = V_sb[:, 520 * kb + 65 * hk:520 * kb + 65 * (hk + 1)]
                            else:
                                sub = 4 * qc + (k - n_pad)
                                vl = vlb[:, 520 * sub + 65 * hk:520 * sub + 65 * (hk + 1)]
                            sT.inc(tensor.matmul(psO[0:65, 0:512], vl, p_bf[pp][:],
                                                 start=(k == 0), stop=(k == n_tot - 1)))

                    @block.scalar
                    def _(scalar, h=h, qc=qc, sbase=sbase, n_pad=n_pad,
                          bT=bT, bE=bE, n_tot=n_tot):
                        for k in range(n_tot):
                            pp = k % 4
                            scalar.wait_ge(sT.h, bT + 2 * k + 1)
                            if k < n_pad:
                                bias = Mbias[:, 32 * (sbase + k) + h:
                                             32 * (sbase + k) + h + 1]
                            else:
                                bias = Mneg[:, h:h + 1]
                            sE.inc(scalar.activation(p_bf[pp][:], psS[pp][:], AF.Exp,
                                                     bias=bias))

                    @block.vector
                    def _(vector, h=h, qc=qc, n_pad=n_pad, bT=bT, bE=bE, bP=bP,
                          n_tot=n_tot):
                        for k in range(n_pad, n_tot):
                            pp = k % 4
                            db = k - n_pad
                            vector.wait_ge(sE.h, bE + k + 1)
                            sP.inc(vector.tensor_tensor(
                                p_bf[pp][:], p_bf[pp][:],
                                rm_bf[:, 512 * db:512 * (db + 1)], op=ALU.mult))

                with nc.Block() as block:
                    @block.vector
                    def _(vector, h=h, qc=qc):
                        vector.wait_ge(sT.h, sT.n)
                        sR.inc(vector.reciprocal(rec[:], psO[64:65, 0:512]))
                        vector.wait_ge(sR.h, sR.n)
                        sR.inc(vector.tensor_copy(o_bf[:], psO[0:64, 0:512]))

                with nc.Block() as block:
                    @block.tensor
                    def _(tensor, h=h, qc=qc):
                        tensor.wait_ge(sR.h, sR.n)
                        smm.inc(tensor.matmul(psB[0:64, 0:512], ones1[0:1, 0:64],
                                              rec[0:1, :], start=True, stop=True))

                    @block.vector
                    def _(vector, h=h, qc=qc):
                        vector.wait_ge(smm.h, smm.n)
                        vector.wait_ge(sst.h, sst.n)
                        sO.inc(vector.tensor_tensor(o_nm[:], o_bf[:],
                                                    psB[0:64, 0:512], op=ALU.mult))

                with nc.Block() as block:
                    @block.sync
                    def _(sync, h=h, qc=qc):
                        sync.wait_ge(sO.h, sO.n)
                        sst.inc(sync.dma_start(
                            out=o_d[64 * h:64 * (h + 1), 512 * qc:512 * (qc + 1)],
                            in_=o_nm[:]), 16)
                        sync.wait_ge(sst.h, sst.n)

    with ExitStack() as es2:
        wo = [es2.enter_context(nc.sbuf_tensor(f"wo{i}", [128, D], BF16))
              for i in range(16)]
        wst = [es2.enter_context(nc.sbuf_tensor(f"ewst{i}", [128, D], F32))
               for i in range(2)]
        ofm = es2.enter_context(nc.sbuf_tensor("ofm", [128, 16 * 128], BF16))
        psW = [es2.enter_context(nc.psum_tensor(f"psW{i}", [128, 512], F32))
               for i in range(2)]
        y_sb = es2.enter_context(nc.sbuf_tensor("y_sb", [128, D], F32))
        lny = es2.enter_context(nc.sbuf_tensor("lny", [128, D], F32))
        xt = es2.enter_context(nc.sbuf_tensor("ext", [128, D], F32))
        o_sb = es2.enter_context(nc.sbuf_tensor("eo_sb", [128, D], F32))
        ln_sc = es2.enter_context(nc.sbuf_tensor("ln_sc", [128, D], BF16))
        ln_stats = es2.enter_context(nc.sbuf_tensor("ln_stats", [128, 8], F32))
        for i in range(16):
            with nc.Block() as block:
                @block.sync
                def _(sync, i=i):
                    if i >= 2:
                        sync.wait_ge(scv.h, scv.n)
                    sd.inc(sync.dma_start(out=wst[i % 2][:],
                                          in_=WoT[128 * i:128 * (i + 1), :]), 16)
                    sync.wait_ge(sd.h, sd.n)

            with nc.Block() as block:
                @block.vector
                def _(vector, i=i):
                    scv.inc(vector.tensor_copy(wo[i][:], wst[i % 2][:]))

        for tt in range(NTILE):
            with nc.Block() as block:
                @block.sync
                def _(sync, tt=tt):
                    sync.wait_ge(sev.h, sev.n)
                    sync.wait_ge(smm.h, smm.n)
                    sd.inc(sync.dma_start(
                        out=ofm[:],
                        in_=o_d.rearrange("(c p) n -> p c n", p=128)[
                            :, :, 128 * tt:128 * (tt + 1)]), 16)
                    sync.wait_ge(sd.h, sd.n)
                    sd.inc(sync.dma_start(out=xt[:],
                                          in_=x2[128 * tt:128 * (tt + 1), :]), 16)
                    sd.wait(sync)

            bmm = smm.n
            bev = sev.n
            with nc.Block() as block:
                @block.tensor
                def _(tensor, tt=tt, bmm=bmm, bev=bev):
                    if tt == 0:
                        tensor.wait_ge(scv.h, scv.n)
                    for fo in range(4):
                        if fo >= 2:
                            tensor.wait_ge(sev.h, bev + fo - 1)
                        for i in range(16):
                            mm = tensor.matmul(
                                psW[fo % 2][:],
                                ofm[:, 128 * i:128 * (i + 1)],
                                wo[i][:, 512 * fo:512 * (fo + 1)],
                                start=(i == 0), stop=(i == 15))
                        smm.inc(mm)

                @block.vector
                def _(vector, tt=tt, bmm=bmm, bev=bev):
                    for fo in range(4):
                        vector.wait_ge(smm.h, bmm + fo + 1)
                        sev.inc(vector.tensor_copy(y_sb[:, 512 * fo:512 * (fo + 1)],
                                                   psW[fo % 2][:]))

            emit_ln_tiles(nc, [y_sb], [lny], 1, D, es2, f"eu{tt}",
                          shared=(ln_sc, ln_stats), sems=(lnv, lns),
                          pre_waits=[(sev, sev.n)])
            with nc.Block() as block:
                @block.vector
                def _(vector, tt=tt):
                    vector.wait_ge(lnv.h, lnv.n)
                    vector.wait_ge(sst.h, sst.n)
                    sev.inc(vector.tensor_tensor(o_sb[:], xt[:], lny[:], op=ALU.add))

            with nc.Block() as block:
                @block.sync
                def _(sync, tt=tt):
                    sync.wait_ge(sev.h, sev.n)
                    sst.inc(sync.dma_start(out=x3[128 * tt:128 * (tt + 1), :],
                                           in_=o_sb[:]), 16)
                    sync.wait_ge(sst.h, sst.n)

        with nc.Block() as block:
            @block.sync
            def _(sync):
                sst.wait(sync)
    es.close()
    return nc


def build_launch3():
    nc = bass.Bass("TRN2", target_bir_lowering=False)
    x3_d = nc.dram_tensor("x3_h", [TPC, D], F32, kind="ExternalOutput").ap()
    ident_shared = nc.dram_tensor("identf", [128, 128], F32, kind="ExternalInput").ap()
    build_l1attn_uni(ncp=NCP(nc, "e_"), io=dict(x3=x3_d))
    build_ghost(ncp=NCP(nc, "f_"), io=dict(x3=x3_d, identf=ident_shared),
                premat=True)
    return nc



GM_W = 33024


def build_ghostmat(ncp=None, io=None):
    nc = ncp if ncp is not None else bass.Bass("TRN2", target_bir_lowering=False)
    IN, OUT = _io_funcs(nc, io)
    gidx = IN("gidx", [128, GM_W], I32)
    lut = IN("lut", [1, 16], F32)
    wpart = OUT("wpart", [128, GM_W], BF16)

    es = ExitStack()
    sd = Sem(es, nc, "sd")
    sst = Sem(es, nc, "sst")
    smm = Sem(es, nc, "smm")
    sev = Sem(es, nc, "sev")
    lut_sb = es.enter_context(nc.sbuf_tensor("lut_sb", [1, 16], F32))
    lut_bc = es.enter_context(nc.sbuf_tensor("lut_bc", [128, 16], F32))
    ones1 = es.enter_context(nc.sbuf_tensor("ones1", [1, 128], F32))
    ist = [es.enter_context(nc.sbuf_tensor(f"ist{i}", [128, 4128], I32))
           for i in range(2)]
    wtile = [es.enter_context(nc.sbuf_tensor(f"wtile{i}", [128, 4128], BF16))
             for i in range(2)]
    wtmp = es.enter_context(nc.sbuf_tensor("wtmp", [128, 4128], BF16))

    with nc.Block() as block:
        @block.sync
        def _(sync):
            sd.inc(sync.dma_start(out=lut_sb[:], in_=lut[:, :]), 16)
            sd.wait(sync)

    with nc.Block() as block:
        @block.vector
        def _(vector):
            sev.inc(vector.memset(ones1[:], 1.0))

    with ExitStack() as es0:
        psb = es0.enter_context(nc.psum_tensor("psb", [128, 512], F32))
        with nc.Block() as block:
            @block.tensor
            def _(tensor):
                tensor.wait_ge(sev.h, sev.n)
                smm.inc(tensor.matmul(psb[:, 0:16], ones1[0:1, :], lut_sb[0:1, :],
                                      start=True, stop=True))

            @block.vector
            def _(vector):
                vector.wait_ge(smm.h, smm.n)
                sev.inc(vector.tensor_copy(lut_bc[:], psb[:, 0:16]))

    NJ = GM_W // 4128
    for j in range(NJ):
        with nc.Block() as block:
            @block.sync
            def _(sync, j=j):
                if j >= 2:
                    sync.wait_ge(sev.h, sev.n)
                sd.inc(sync.dma_start(out=ist[j % 2][:],
                                      in_=gidx[:, 4128 * j:4128 * (j + 1)]), 16)
                sync.wait_ge(sd.h, sd.n)

        with nc.Block() as block:
            @block.vector
            def _(vector, j=j):
                vector.wait_ge(sst.h, sst.n)
                sev.inc(vector.tensor_scalar(wtile[j % 2][:], ist[j % 2][:],
                                             0.0, lut_bc[:, 0:1],
                                             op0=ALU.is_equal, op1=ALU.mult))
                for v in range(1, 16):
                    sev.inc(vector.tensor_scalar(wtmp[:], ist[j % 2][:],
                                                 float(v), lut_bc[:, v:v + 1],
                                                 op0=ALU.is_equal, op1=ALU.mult))
                    vector.wait_ge(sev.h, sev.n)
                    sev.inc(vector.tensor_tensor(wtile[j % 2][:], wtile[j % 2][:],
                                                 wtmp[:], op=ALU.add))
                    vector.wait_ge(sev.h, sev.n)

        with nc.Block() as block:
            @block.sync
            def _(sync, j=j):
                sync.wait_ge(sev.h, sev.n)
                sst.inc(sync.dma_start(out=wpart[:, 4128 * j:4128 * (j + 1)],
                                       in_=wtile[j % 2][:]), 16)
                sync.wait_ge(sst.h, sst.n)

    with nc.Block() as block:
        @block.sync
        def _(sync):
            sst.wait(sync)
    es.close()
    return nc
